# version 09
# speedup vs baseline: 1.7248x; 1.0627x over previous
import contextlib
import numpy as np
import ml_dtypes
import concourse.bass as bass
import concourse.mybir as mybir
from concourse.bass_utils import run_bass_kernel_spmd
from concourse.alu_op_type import AluOpType as ALU

F32 = mybir.dt.float32
BF16 = mybir.dt.bfloat16
AF = mybir.ActivationFunctionType
AX = mybir.AxisListType
BFNP = ml_dtypes.bfloat16

T = 2048
NT = 16
DM = 1024
NCH = 8
NH = 16
HD = 64
NKV = 4
HID = 2048
NE = 8
NG = 4
EPS_LN = 1e-5
EPS_A = float(np.finfo(np.float32).eps)
EPS_F = 1e-6
NEGBIG = -1e30
NSEQ = 2

AR_COLS = 44032
O_QPK = 0
O_KPK = 8192
O_VAG = 16384
O_W = 24704
O_SCR = 36992


class Eng:
    def __init__(self, prog, key, selfchain=False, dma=False):
        self.prog = prog
        self.key = key
        self.selfchain = selfchain
        self.dma = dma
        self.n = 0
        self.waited = {}

    def wait(self, tok):
        if tok is None:
            return
        key, val = tok
        if val <= 0 or self.waited.get(key, 0) >= val:
            return
        self.waited[key] = val
        sems = self.prog.sems

        def thunk(g, key=key, val=val):
            g.wait_ge(sems[key], val)
        self.prog.streams[self.key].append(thunk)

    def op(self, meth, *a, **kw):
        if self.selfchain and self.n > 0:
            self.wait((self.key, self.n))
        d = 16 if self.dma else 1
        self.n += d
        sems = self.prog.sems

        def thunk(g, meth=meth, a=a, kw=kw, key=self.key, d=d):
            getattr(g, meth)(*a, **kw).then_inc(sems[key], d)
        self.prog.streams[self.key].append(thunk)
        return (self.key, self.n)

    def tok(self):
        return (self.key, self.n)


class DmaQ:

    def __init__(self, prog, qkey):
        self.prog = prog
        self.qkey = qkey
        self.chan = {}

    def wait(self, tok):
        if tok is None:
            return
        key, val = tok
        if val <= 0:
            return
        wdict = self.chan.setdefault("_waited", {})
        if isinstance(wdict, dict) and wdict.get(key, 0) >= val:
            return
        wdict[key] = val
        sems = self.prog.sems

        def thunk(g, key=key, val=val):
            g.wait_ge(sems[key], val)
        self.prog.streams[self.qkey].append(thunk)

    def op(self, chan, meth, *a, **kw):
        key = f"{self.qkey}.{chan}"
        if key not in self.prog.sems:
            self.prog.newsem(key)
            self.chan[key] = 0
        self.chan[key] += 16
        n = self.chan[key]
        sems = self.prog.sems

        def thunk(g, meth=meth, a=a, kw=kw, key=key):
            getattr(g, meth)(*a, **kw).then_inc(sems[key], 16)
        self.prog.streams[self.qkey].append(thunk)
        return (key, n)

    def toks(self):
        return [(k, v) for k, v in self.chan.items() if k != "_waited"]


class Prog:
    def __init__(self, nc, dbg=()):
        self.nc = nc
        self.streams = {k: [] for k in ("pe", "dve", "act", "sp", "pl")}
        self.sems = {}
        self.newsem = None
        self.dbg = set(dbg)
        self.dbg_tensors = {}
        self.pe = Eng(self, "pe")
        self.dve = Eng(self, "dve", selfchain=True)
        self.act = Eng(self, "act", selfchain=True)
        self.sp = DmaQ(self, "sp")
        self.pl = DmaQ(self, "pl")

    def barrier(self):
        engs = (self.pe, self.dve, self.act)
        toks = [e.tok() for e in engs] + self.sp.toks() + self.pl.toks()
        for e in engs:
            for t in toks:
                if t[0] != e.key:
                    e.wait(t)
        for q in (self.sp, self.pl):
            for t in toks:
                q.wait(t)

    def tap(self, name, ap, producer_tok):
        if name not in self.dbg or name in self.dbg_tensors:
            return
        t = self.nc.dram_tensor("dbg_" + name, list(ap.shape), ap.dtype,
                                kind="ExternalOutput").ap()
        self.dbg_tensors[name] = t
        self.sp.wait(producer_tok)
        self.sp.op("dbg", "dma_start", out=t, in_=ap)


def build_program(nphases=5, dbg=(), nseq=NSEQ):
    nc = bass.Bass("TRN2", target_bir_lowering=False)
    P = Prog(nc, dbg)

    def din(name, shape, dtype=F32):
        return nc.dram_tensor(name, shape, dtype, kind="ExternalInput").ap()

    x_d = din("x", [NSEQ, T, DM])
    out_d = nc.dram_tensor("out", [NSEQ, T, DM], F32, kind="ExternalOutput").ap()
    x1_sp = nc.dram_tensor("x1_spill", [NSEQ, T, DM], F32, kind="Internal").ap()
    xin_sp = nc.dram_tensor("xin_spill", [NSEQ, T, DM], F32, kind="Internal").ap()

    waq_d = din("L0_attn_Wq", [DM, DM])
    wakv_d = din("L0_attn_Wkv", [DM, 2 * DM])
    wao_d = din("L0_attn_Wout", [DM, DM])
    wag_d = din("L0_attn_Wg", [DM, NH])
    wfq_d = din("L0_ffn_Wq", [DM, DM])
    wfk_d = din("L0_ffn_Wk", [DM, 256])
    wfv_d = din("L0_ffn_Wv", [DM, 256])
    wfg_d = din("L0_ffn_Wg", [HD, HD])
    wfo_d = din("L0_ffn_Wo", [DM, DM])
    w1_d = din("L0_ffn_W1", [NE, DM, 2 * HID])
    w2_d = din("L0_ffn_W2", [NE, HID, DM])
    b2_d = din("L0_ffn_b2", [NE, DM])
    normw_d = din("d_normw_bc", [128, DM])
    ln1_d = din("d_ln1_bc", [128, DM])
    ln2_d = din("d_ln2_bc", [128, DM])
    qn_d = din("d_qn_bc", [128, HD])
    kn_d = din("d_kn_bc", [128, HD])
    br_d = din("d_br_bc", [128, NE])
    bo_d = din("d_bo_col", [128, NCH])
    bgc_d = din("d_bg_col", [HD, 1])
    b1c_d = din("d_b1_cols", [NE, 128, 32])
    wr_d = din("d_wr_cols", [128, 64])
    bq_d = din("d_bq_row", [1, DM])
    bk_d = din("d_bk_row", [1, 256])
    bv_d = din("d_bv_row", [1, 256])
    identf_d = din("d_identf", [128, 128])
    identb_d = din("d_identb", [128, 128], BF16)
    ulo_d = din("d_ulo", [128, 128], BF16)
    cos_d = din("d_cos_t", [128, NT * 32])
    sin_d = din("d_sin_t", [128, NT * 32])

    ctx = contextlib.ExitStack()

    def sbuf(name, shape, dt=F32):
        return ctx.enter_context(nc.sbuf_tensor(name, shape, dt))

    def psum(name, shape, dt=F32):
        return ctx.enter_context(nc.psum_tensor(name, shape, dt))

    with ctx:
        identf = sbuf("identf", [128, 128])
        identb = sbuf("identb", [128, 128], BF16)
        ulo = sbuf("ulo", [128, 128], BF16)
        ones_bf = sbuf("ones_bf", [65, 512], BF16)
        negone_bf = sbuf("negone_bf", [65, 128], BF16)
        ones_f = sbuf("ones_f", [128, 64])
        normw_bc = sbuf("normw_bc", [128, DM])
        ln1_bc = sbuf("ln1_bc", [128, DM])
        ln2_bc = sbuf("ln2_bc", [128, DM])
        qn_bc = sbuf("qn_bc", [128, HD])
        kn_bc = sbuf("kn_bc", [128, HD])
        br_bc = sbuf("br_bc", [128, NE])
        bo_col = sbuf("bo_col", [128, NCH])
        bg_col = sbuf("bg_col", [HD, 1])
        cos_t = sbuf("cos_t", [128, NT * 32], BF16)
        sin_t = sbuf("sin_t", [128, NT * 32], BF16)
        wr_sb = sbuf("wr_sb", [128, 64])
        b2_bf = sbuf("b2_bf", [NE, DM], BF16)
        bq_bf = sbuf("bq_bf", [1, DM], BF16)
        bk_bf = sbuf("bk_bf", [1, 256], BF16)
        bv_bf = sbuf("bv_bf", [1, 256], BF16)
        wgf_bf = sbuf("wgf_bf", [HD, HD], BF16)
        b1c = [sbuf("b1c0", [128, 32]), sbuf("b1c1", [128, 32])]
        mask_fm = sbuf("mask_fm", [NE, T], BF16)
        g_sb_t = sbuf("g_sb", [NH, T], BF16)
        pen_fm = sbuf("pen_fm", [NE, T], BF16)
        AR = sbuf("arena", [128, AR_COLS])
        PS = [psum(f"ps{i}", [128, 1024]) for i in range(4)]

        def arf(off, ncols, p0=0, p=128):
            assert 0 <= off and off + ncols <= AR_COLS, (off, ncols)
            return AR[p0:p0 + p, off:off + ncols]

        def arb(off, nbf, p0=0, p=128):
            assert 0 <= off and off + (nbf + 1) // 2 <= AR_COLS, (off, nbf)
            return AR[p0:p0 + p, off:off + (nbf + 1) // 2].bitcast(BF16)

        def newsem(key):
            P.sems[key] = ctx.enter_context(
                nc.semaphore("sem_" + key.replace(".", "_")))
        P.newsem = newsem
        for k in ("pe", "dve", "act"):
            newsem(k)
        pe, dve, act, sp, pl = P.pe, P.dve, P.act, P.sp, P.pl
        S = {}

        for dst, src in [(identf, identf_d), (identb, identb_d), (ulo, ulo_d),
                         (normw_bc, normw_d), (ln1_bc, ln1_d), (ln2_bc, ln2_d),
                         (qn_bc, qn_d), (kn_bc, kn_d), (br_bc, br_d),
                         (bo_col, bo_d), (bg_col, bgc_d), (wr_sb, wr_d)]:
            sp.op("init", "dma_start", out=dst[:], in_=src)
        for dst, src in [(cos_t, cos_d), (sin_t, sin_d)]:
            pl.op("binit", "dma_start", out=dst[:], in_=src)
        for dst, src in [(b2_bf, b2_d), (bq_bf, bq_d), (bk_bf, bk_d),
                         (bv_bf, bv_d), (wgf_bf, wfg_d)]:
            pl.op("binit", "dma_start", out=dst[:], in_=src)
        dve.op("memset", ones_bf[:], 1.0)
        dve.op("memset", negone_bf[:], -1.0)
        dve.op("memset", ones_f[:], 1.0)
        dve.op("memset",
               arb(O_VAG, 16 * NH * 65).rearrange("p (j o) -> p j o", o=65)[:, :, HD:],
               1.0)
        P.barrier()

        def ln_stats(src_ap, st12, mv, ncols=1024):
            nhh = (ncols + 511) // 512
            for hh in range(nhh):
                dve.op("bn_stats", st12[:, 6 * hh:6 * hh + 6],
                       src_ap[:, hh * 512:min(ncols, (hh + 1) * 512)])
            return dve.op("bn_aggr", mv, st12[:, 0:6 * nhh])

        def rstd_of(var_ap, out_ap, eps, scale2=1.0):
            dve.op("tensor_scalar", out_ap, var_ap, float(eps), None, ALU.add)
            t = dve.op("reciprocal", out_ap, out_ap)
            act.wait(t)
            if scale2 == 1.0:
                return act.op("sqrt", out_ap, out_ap)
            return act.op("activation", out_ap, out_ap, AF.Sqrt, scale=float(scale2))

        def entry_norm(buf, mv, sc, eps2):
            rstd = sc[:, 0:1]
            c2 = sc[:, 1:2]
            t_r1 = rstd_of(mv[:, 1:2], rstd, EPS_LN)
            dve.wait(t_r1)
            dve.op("tensor_tensor", c2, rstd, rstd, ALU.mult)
            dve.op("tensor_scalar", c2, c2, -EPS_LN, 1.0 + eps2, ALU.mult, ALU.add)
            t = dve.op("reciprocal", c2, c2)
            act.wait(t)
            t_s2 = act.op("sqrt", c2, c2)
            dve.wait(t_s2)
            return dve.op("tensor_tensor", sc[:, 0:1], c2, rstd, ALU.mult)

        def phase1(s):
            wqa = arb(O_W, 8192)
            wkv = arb(O_W + 4096, 16384)
            wga = arb(O_SCR + 6688, 128)
            t_wq = pl.op("w0", "dma_start", out=wqa.rearrange("p (c n) -> p c n", n=DM),
                         in_=waq_d.rearrange("(c p) n -> p c n", p=128))
            t_wkv = pl.op("w1", "dma_start", out=wkv.rearrange("p (c n) -> p c n", n=2 * DM),
                          in_=wakv_d.rearrange("(c p) n -> p c n", p=128))
            t_wg = pl.op("wg", "dma_start", out=wga.rearrange("p (c n) -> p c n", n=NH),
                         in_=wag_d.rearrange("(c p) n -> p c n", p=128))
            x0b = [arf(O_SCR, 1024), arf(O_SCR + 1024, 1024)]
            abf = [arb(O_SCR + 2048, 1024), arb(O_SCR + 2560, 1024)]
            afm_g = arb(O_SCR + 3072, 4096)
            tmp_f = arf(O_SCR + 5120, 512)
            st12 = arf(O_SCR + 5632, 12)
            mv = arf(O_SCR + 5644, 2)
            sc = arf(O_SCR + 5648, 4)
            g_bf = g_sb_t
            q_pk = arb(O_QPK, 16384)
            k_pk = arb(O_KPK, 16384)
            vaug3 = arb(O_VAG, 16 * NH * 65).rearrange("p (j h d) -> p j h d",
                                                       h=NH, d=65)
            dve.op("memset",
                   arb(O_VAG, 16 * NH * 65).rearrange("p (j o) -> p j o",
                                                      o=65)[:, :, HD:], 1.0)
            t_xuse = [None, None]
            abf_guard = [None, None]
            tr_ev = [None, None]
            qk_ev = [None, None]
            v_ev = [None, None]
            t_gev = None
            afm_pe_guard = None
            for g in range(NG):
                for it in range(4):
                    i = 4 * g + it
                    buf = x0b[i % 2]
                    sp.wait(t_xuse[i % 2])
                    t_x0 = sp.op(f"x0{i % 2}", "dma_start", out=buf,
                                 in_=x_d[s, i * 128:(i + 1) * 128, :])
                    dve.wait(t_x0)
                    ln_stats(buf, st12, mv)
                    t_c = entry_norm(buf, mv, sc, EPS_A)
                    dve.wait(abf_guard[i % 2])
                    for hh in range(2):
                        dve.op("tensor_scalar", tmp_f,
                               buf[:, hh * 512:(hh + 1) * 512],
                               mv[:, 0:1], sc[:, 0:1], ALU.subtract, ALU.mult)
                        t_a = dve.op("tensor_tensor",
                                     abf[i % 2][:, hh * 512:(hh + 1) * 512], tmp_f,
                                     normw_bc[:, hh * 512:(hh + 1) * 512], ALU.mult)
                    t_xuse[i % 2] = t_a
                    dve.wait(afm_pe_guard)
                    t_t = None
                    for c in range(NCH):
                        slot = c % 2
                        trp = (PS[3][:, 0:64] if slot == 0
                               else PS[3][:, 512:576]).bitcast(BF16)
                        pe.wait(t_a)
                        pe.wait(tr_ev[slot])
                        t_t = pe.op("transpose", trp,
                                    abf[i % 2][:, c * 128:(c + 1) * 128], identb[:])
                        dve.wait(t_t)
                        tr_ev[slot] = dve.op(
                            "tensor_copy",
                            afm_g[:, c * 512 + it * 128: c * 512 + (it + 1) * 128],
                            trp)
                    abf_guard[i % 2] = t_t
                pe.wait(t_wq)
                pe.wait(t_wkv)
                pe.wait(tr_ev[0])
                pe.wait(tr_ev[1])
                for which in ("q", "k"):
                    wsb, dstpk, stride = ((wqa, q_pk, DM) if which == "q"
                                          else (wkv, k_pk, 2 * DM))
                    for j in range(NH // 2):
                        slot = j % 2
                        ps = PS[0][:, slot * 512:(slot + 1) * 512]
                        pe.wait(qk_ev[slot])
                        for half in range(2):
                            n0 = j * 128 + half * 64
                            tp = (0, 64) if half == 1 else None
                            outap = ps[64:128, :] if half == 1 else ps[0:64, :]
                            for c in range(NCH):
                                t_mm = pe.op(
                                    "matmul", outap,
                                    wsb[:, c * stride + n0: c * stride + n0 + 64],
                                    afm_g[:, c * 512:(c + 1) * 512],
                                    start=(c == 0),
                                    stop=(c == NCH - 1),
                                    tile_position=tp)
                        dst = dstpk[:, j * 2048 + g * 512: j * 2048 + (g + 1) * 512]
                        if which == "q":
                            act.wait(t_mm)
                            qk_ev[slot] = act.op("activation", dst, ps, AF.Copy,
                                                 scale=0.125)
                        else:
                            dve.wait(t_mm)
                            qk_ev[slot] = dve.op("tensor_copy", dst, ps)
                for it in range(4):
                    i = 4 * g + it
                    for dg in range(2):
                        ps = PS[1][:, dg * 512:(dg + 1) * 512]
                        pe.wait(v_ev[dg])
                        for c in range(NCH):
                            t_mm = pe.op("matmul", ps,
                                         afm_g[:, c * 512 + it * 128:
                                               c * 512 + (it + 1) * 128],
                                         wkv[:, c * 2048 + 1024 + dg * 512:
                                             c * 2048 + 1024 + (dg + 1) * 512],
                                         start=(c == 0), stop=(c == NCH - 1))
                        dve.wait(t_mm)
                        v_ev[dg] = dve.op(
                            "tensor_copy", vaug3[:, i, dg * 8:(dg + 1) * 8, 0:64],
                            ps.rearrange("p (h d) -> p h d", d=64))
                pe.wait(t_wg)
                if t_gev is not None:
                    pe.wait(t_gev)
                psg = PS[2][0:16, 0:512]
                for c in range(NCH):
                    t_mm = pe.op("matmul", psg, wga[:, c * 16:(c + 1) * 16],
                                 afm_g[:, c * 512:(c + 1) * 512],
                                 start=(c == 0), stop=(c == NCH - 1))
                act.wait(t_mm)
                t_gev = act.op("activation", g_bf[:, g * 512:(g + 1) * 512], psg,
                               AF.Sigmoid)
                afm_pe_guard = pe.tok()
            S["g_bf"] = g_bf
            S["t_g"] = t_gev
            P.tap("q_pk", q_pk[:], act.tok())
            P.tap("k_pk", k_pk[:], dve.tok())
            P.tap("v_aug", arb(O_VAG, 16 * NH * 65)[:], dve.tok())
            P.tap("g_bf", g_bf[:], act.tok())

        def attention(s, blk):
            q_pk = arb(O_QPK, 16384)
            if blk == "A":
                k_pk = arb(O_KPK, 16384)
                vaug3 = arb(O_VAG, 16 * NH * 65).rearrange(
                    "p (j h d) -> p j h d", h=NH, d=65)
                wz = arb(O_W, 8192)
                O_grp = arb(O_W + 4096, 4096)
                g_bf = S["g_bf"]
            else:
                k_pk = arb(O_KPK, 8192)
                vaug3 = arb(O_KPK + 4096, 16 * NKV * 65).rearrange(
                    "p (j h d) -> p j h d", h=NKV, d=65)
                wz = arb(O_W + 8192, 8192)
                O_grp = arb(O_W, 4096)
            z_sb = arf(O_W + 8192, 4096) if blk == "A" else arf(O_W + 4096, 4096)
            pbuf = [arb(O_SCR, 1024), arb(O_SCR + 512, 1024)]
            rrow = arf(O_SCR + 1024, 512, p=65)
            bcr_sb = arf(O_SCR + 1536, 512, p=64)
            t1_sb = arf(O_SCR + 2048, 512, p=64)
            on_sb = arb(O_SCR + 2560, 512, p=64)
            gate_sb = arf(O_SCR + 2816, 512, p=64)
            grow_b = [arb(O_SCR + 2816, 512, p=1), arb(O_SCR + 3072, 512, p=1),
                      arb(O_SCR + 6472, 512, p=1), arb(O_SCR + 6728, 512, p=1)]
            st12 = arf(O_SCR + 3344, 12)
            mv = arf(O_SCR + 3356, 2)
            sc = arf(O_SCR + 3358, 6)
            q2s = arf(O_SCR + 3368, 512)
            mrow_b = [arb(O_SCR + 3880, 512, p=1), arb(O_SCR + 4136, 512, p=1)]
            km_sc = arf(O_SCR + 4392, 32, p=1)
            x0b = [arf(O_SCR + 4424, 1024), arf(O_SCR + 5448, 1024)]
            h2T = arf(O_VAG, 1024)
            lgs = arf(O_VAG + 1024, 128, p=8)
            m8 = arf(O_VAG + 1160, 24)
            h2f = arf(O_VAG + 1192, 1024)

            pool_g = [None, None]
            pvu = [None, None]
            mrd = [None, None]
            q2s_g = None
            prow_g = None
            g_use = [None, None, None, None]
            gd_tok = [None, None, None, None]
            x0_use = [None, None]
            on_tok = {}
            t_bce = None
            ps2b1_guard = None
            ps3g = [None, None]
            on_guard = None
            t_zdone = [None, None]
            t_comb2 = [None, None]
            t_routr = None
            t_mb_pipe = {}

            def head_geom(h):
                kh = h if blk == "A" else h // 4
                qbase, qrow = (h // 2) * 2048, (h % 2) * 64
                kbase = qbase if blk == "A" else kh * 2048
                return kh, qbase, qrow, kbase, qrow

            def stok_mrow(g, h):
                nonlocal q2s_g, prow_g, t_bce
                kh, qbase, qrow, kbase, krow = head_geom(h)
                mslot = h % 2
                if g == 0 and (blk == "A" or h % 4 == 0):
                    for c4 in range(4):
                        act.wait(q2s_g)
                        t_q2 = act.op("activation", q2s[krow:krow + 64, :],
                                      k_pk[krow:krow + 64,
                                           kbase + c4 * 512: kbase + (c4 + 1) * 512],
                                      AF.Square)
                        pe.wait(t_q2)
                        pe.wait(t_bce)
                        pe.wait(prow_g)
                        t_k2 = pe.op("matmul", PS[2][0:1, 0:512],
                                     ones_f[krow:krow + 64, 0:1],
                                     q2s[krow:krow + 64, :],
                                     start=True, stop=True,
                                     tile_position=(krow, 0) if krow else None)
                        q2s_g = t_k2
                        dve.wait(t_k2)
                        if c4 == 0:
                            t_km = dve.op("reduce_max", km_sc[0:1, 16:17],
                                          PS[2][0:1, 0:512], AX.X)
                        else:
                            dve.op("reduce_max", km_sc[0:1, 17:18],
                                   PS[2][0:1, 0:512], AX.X)
                            t_km = dve.op("tensor_tensor", km_sc[0:1, 16:17],
                                          km_sc[0:1, 16:17], km_sc[0:1, 17:18],
                                          ALU.max)
                        prow_g = t_km
                    t_km = dve.op("tensor_scalar",
                                  km_sc[0:1, (h if blk == "A" else 4 * kh):
                                        (h if blk == "A" else 4 * kh) + 1],
                                  km_sc[0:1, 16:17], 1.0 / HD, None, ALU.mult)
                    S[f"km_{blk}_{kh}"] = t_km
                act.wait(q2s_g)
                t_q2 = act.op("activation", q2s[qrow:qrow + 64, :],
                              q_pk[qrow:qrow + 64,
                                   qbase + g * 512: qbase + (g + 1) * 512],
                              AF.Square)
                pe.wait(t_q2)
                pe.wait(t_bce)
                pe.wait(prow_g)
                t_mm = pe.op("matmul", PS[2][0:1, 0:512],
                             ones_f[qrow:qrow + 64, 0:1],
                             q2s[qrow:qrow + 64, :], start=True, stop=True,
                             tile_position=(qrow, 0) if qrow else None)
                q2s_g = t_mm
                act.wait(t_mm)
                act.wait(mrd[mslot])
                act.wait(S.get(f"km_{blk}_{kh}"))
                kmcol = h if blk == "A" else 4 * kh
                t_mb = act.op("activation", mrow_b[mslot][0:1, :],
                              PS[2][0:1, 0:512], AF.Sqrt,
                              scale=km_sc[0:1, kmcol:kmcol + 1])
                prow_g = t_mb
                if blk == "A":
                    gslot = h % 4
                    sp.wait(S["t_g"])
                    sp.wait(g_use[gslot])
                    gd_tok[gslot] = sp.op(
                        f"gr{gslot}", "dma_start",
                        out=grow_b[gslot][0:1, :],
                        in_=g_bf[h:h + 1, g * 512:(g + 1) * 512])
                return t_mb

            def sprime_pv(g, h, t_mb):
                kh, qbase, qrow, kbase, krow = head_geom(h)
                tp_q = (qrow, 0) if qrow else None
                mslot = h % 2
                njc = 4 * g + 4
                t_pv = None
                t_mm = None
                pend = None
                pairs = [list(range(j0, min(j0 + 2, njc))) for j0 in range(0, njc, 2)]
                for pi, jpair in enumerate(pairs):
                    sl = S.get("pool_slot", 0)
                    S["pool_slot"] = 1 - sl
                    pe.wait(pool_g[sl])
                    pe.wait(t_mb)
                    coffs = []
                    for idx, j in enumerate(jpair):
                        coff = max(0, j * 128 - g * 512)
                        coffs.append(coff)
                        ps = PS[sl][:, idx * 512: idx * 512 + 512]
                        diag = (j * 128 >= g * 512)
                        pe.op("matmul", ps[:, coff:512],
                              k_pk[krow:krow + 64,
                                   kbase + j * 128: kbase + (j + 1) * 128],
                              q_pk[qrow:qrow + 64,
                                   qbase + g * 512 + coff: qbase + (g + 1) * 512],
                              start=True, stop=False, tile_position=tp_q)
                        t_mm = pe.op("matmul", ps[:, coff:512], negone_bf[0:1, :],
                                     mrow_b[mslot][0:1, coff:512],
                                     start=False, stop=not diag)
                        if diag:
                            t_mm = pe.op("matmul", ps[:, coff:coff + 128],
                                         identb[:], ulo[:], start=False, stop=True)
                    act.wait(t_mm)
                    act.wait(pvu[sl])
                    if len(jpair) == 2 and coffs[0] == 0 and coffs[1] == 0:
                        t_exp = act.op("activation", pbuf[sl][:, 0:1024],
                                       PS[sl][:, 0:1024], AF.Exp)
                    else:
                        for idx, j in enumerate(jpair):
                            t_exp = act.op(
                                "activation",
                                pbuf[sl][:, idx * 512 + coffs[idx]:
                                         idx * 512 + 512],
                                PS[sl][:, idx * 512 + coffs[idx]:
                                       idx * 512 + 512], AF.Exp)
                    pool_g[sl] = t_exp

                    ob = (h % 2) * 512

                    def emit_pv(pend):
                        jp, cf, psl, tex = pend
                        pe.wait(tex)
                        if jp[0] == 0:
                            pe.wait(ps3g[h % 2])
                        tpv = None
                        for idx, j in enumerate(jp):
                            tpv = pe.op("matmul", PS[3][0:65, ob + cf[idx]:ob + 512],
                                        vaug3[:, j, kh, :],
                                        pbuf[psl][:, idx * 512 + cf[idx]:
                                                 idx * 512 + 512],
                                        start=(j == 0), stop=(j == njc - 1))
                        pvu[psl] = tpv
                        return tpv
                    if pend is not None:
                        t_pv = emit_pv(pend)
                    pend = (jpair, coffs, sl, t_exp)
                t_pv = emit_pv(pend)
                mrd[mslot] = t_mm
                return t_pv

            def epilogue(g, h, t_pv):
                nonlocal t_bce, ps2b1_guard, on_guard
                kh, qbase, qrow, kbase, krow = head_geom(h)
                mslot = h % 2
                ob = (h % 2) * 512
                dve.wait(t_pv)
                t_rc = dve.op("reciprocal", rrow[64:65, :],
                              PS[3][64:65, ob:ob + 512])
                pe.wait(t_rc)
                pe.wait(t_bce)
                pe.wait(prow_g)
                t_bc = pe.op("matmul", PS[2][0:64, 0:512], ones_f[64:65, 0:64],
                             rrow[64:65, :], start=True, stop=True)
                dve.wait(t_bc)
                t_bce = dve.op("tensor_copy", bcr_sb, PS[2][0:64, 0:512])
                if blk == "A":
                    gslot = h % 4
                    pe.wait(gd_tok[gslot])
                    pe.wait(ps2b1_guard)
                    t_bg = pe.op("matmul", PS[2][0:64, 512:1024],
                                 ones_bf[0:1, 0:64], grow_b[gslot][0:1, :],
                                 start=True, stop=True)
                    dve.wait(on_guard)
                    t_t1 = dve.op("tensor_tensor", t1_sb, PS[3][0:64, ob:ob + 512],
                                  bcr_sb, ALU.mult)
                    ps3g[h % 2] = t_t1
                    dve.wait(t_bg)
                    t_on = dve.op("tensor_tensor", on_sb, t1_sb,
                                  PS[2][0:64, 512:1024], ALU.mult)
                    ps2b1_guard = t_on
                    g_use[gslot] = t_on
                else:
                    dve.wait(on_guard)
                    t_on = dve.op("tensor_tensor", on_sb, PS[3][0:64, ob:ob + 512],
                                  bcr_sb, ALU.mult)
                    ps3g[h % 2] = t_on
                    pe.wait(t_on)
                    pe.wait(ps2b1_guard)
                    t_gl = pe.op("matmul", PS[2][0:64, 512:1024], wgf_bf[:],
                                 on_sb, start=True, stop=True)
                    act.wait(t_gl)
                    t_sg = act.op("activation", gate_sb, PS[2][0:64, 512:1024],
                                  AF.Sigmoid, bias=bg_col[:])
                    ps2b1_guard = t_sg
                    dve.wait(t_sg)
                    t_on = dve.op("tensor_tensor", on_sb, on_sb, gate_sb,
                                  ALU.mult)
                pcol = (h // 2) * 512
                if h % 2 == 0:
                    on_tok[h] = dve.op(
                        "tensor_copy", O_grp[0:64, pcol:pcol + 512], on_sb)
                else:
                    pe.wait(t_on)
                    pe.wait(t_bce)
                    pe.wait(prow_g)
                    t_rl = pe.op("matmul", PS[2][64:128, 0:512],
                                 identb[0:64, 0:64], on_sb,
                                 start=True, stop=True, tile_position=(0, 64))
                    on_guard = t_rl
                    dve.wait(t_rl)
                    on_tok[h] = dve.op(
                        "tensor_copy",
                        O_grp[64:128, pcol:pcol + 512], PS[2][64:128, 0:512])
                    t_bce = on_tok[h]

            pending = stok_mrow(0, 0)
            pend_ep = None
            for g in range(NG):
                for h in range(NH):
                    cur = pending
                    if h + 1 < NH:
                        pending = stok_mrow(g, h + 1)
                    elif g + 1 < NG:
                        pending = stok_mrow(g + 1, 0)
                    t_pv = sprime_pv(g, h, cur)
                    if pend_ep is not None:
                        epilogue(*pend_ep)
                    pend_ep = (g, h, t_pv)
                epilogue(*pend_ep)
                pend_ep = None
                pe.wait(pool_g[0])
                pe.wait(pool_g[1])
                for mchunk in range(NCH):
                    zsl = mchunk % 2
                    ps = PS[zsl][:, 0:512]
                    pe.wait(t_zdone[zsl])
                    for pair in range(NH // 2):
                        pe.wait(on_tok[2 * pair])
                        pe.wait(on_tok[2 * pair + 1])
                        t_mm = pe.op("matmul", ps,
                                     wz[:, pair * 1024 + mchunk * 128:
                                        pair * 1024 + (mchunk + 1) * 128],
                                     O_grp[:, pair * 512:(pair + 1) * 512],
                                     start=(pair == 0), stop=(pair == NH // 2 - 1))
                    if blk == "A":
                        dve.wait(t_mm)
                        t_ze = dve.op("tensor_copy",
                                      z_sb[:, mchunk * 512:(mchunk + 1) * 512], ps)
                    else:
                        act.wait(t_mm)
                        t_ze = act.op("add",
                                      z_sb[:, mchunk * 512:(mchunk + 1) * 512],
                                      ps, bo_col[:, mchunk:mchunk + 1])
                    t_zdone[zsl] = t_ze
                pool_g[0] = t_zdone[0]
                pool_g[1] = t_zdone[1]
                for it in range(4):
                    i = 4 * g + it
                    psx = 2 if it % 2 == 0 else 3
                    zt = PS[psx][:, 0:1024]
                    pe.wait(t_comb2[psx - 2])
                    if psx == 2:
                        pe.wait(t_bce)
                        pe.wait(ps2b1_guard)
                        pe.wait(prow_g)
                    else:
                        pe.wait(ps3g[0])
                        pe.wait(ps3g[1])
                    t_tr = None
                    for c in range(NCH):
                        pe.wait(t_zdone[c % 2])
                        t_tr = pe.op("transpose", PS[psx][:, c * 128:(c + 1) * 128],
                                     z_sb[:, c * 512 + it * 128:
                                          c * 512 + (it + 1) * 128], identf[:])
                    buf = x0b[it % 2]
                    sp.wait(x0_use[it % 2])
                    t_x = sp.op(f"x0{it % 2}", "dma_start", out=buf,
                                in_=(x_d[s, i * 128:(i + 1) * 128, :] if blk == "A"
                                     else x1_sp[s, i * 128:(i + 1) * 128, :]))
                    dve.wait(t_tr)
                    dve.wait(t_x)
                    if blk == "A":
                        ln_stats(zt, st12, mv)
                        t_rs = rstd_of(mv[:, 1:2], sc[:, 0:1], EPS_LN)
                        dve.wait(t_rs)
                        dve.op("tensor_tensor", sc[:, 1:2], mv[:, 0:1],
                               sc[:, 0:1], ALU.mult)
                        for hh in range(2):
                            dve.op("tensor_scalar", buf[:, hh * 512:(hh + 1) * 512],
                                   buf[:, hh * 512:(hh + 1) * 512],
                                   sc[:, 1:2], None, ALU.subtract)
                            t_c = dve.op("scalar_tensor_tensor",
                                         buf[:, hh * 512:(hh + 1) * 512],
                                         zt[:, hh * 512:(hh + 1) * 512],
                                         sc[:, 0:1],
                                         buf[:, hh * 512:(hh + 1) * 512],
                                         ALU.mult, ALU.add)
                        sp.wait(t_c)
                        t_spl = sp.op(f"sx1{it % 2}", "dma_start",
                                      out=x1_sp[s, i * 128:(i + 1) * 128, :],
                                      in_=buf)
                        x0_use[it % 2] = t_spl
                        t_comb2[psx - 2] = t_c
                        if psx == 3:
                            ps3g[0] = ps3g[1] = t_c
                    else:
                        ln_stats(buf, st12, mv)
                        t_rs = rstd_of(mv[:, 1:2], sc[:, 0:1], EPS_LN)
                        dve.wait(t_rs)
                        for hh in range(2):
                            dve.op("tensor_scalar", buf[:, hh * 512:(hh + 1) * 512],
                                   buf[:, hh * 512:(hh + 1) * 512],
                                   mv[:, 0:1], sc[:, 0:1], ALU.subtract, ALU.mult)
                            t_c = dve.op("tensor_tensor",
                                         buf[:, hh * 512:(hh + 1) * 512],
                                         zt[:, hh * 512:(hh + 1) * 512],
                                         buf[:, hh * 512:(hh + 1) * 512], ALU.add)
                        sp.wait(t_c)
                        t_spl = sp.op(f"sxin{it % 2}", "dma_start",
                                      out=xin_sp[s, i * 128:(i + 1) * 128, :],
                                      in_=buf)
                        x0_use[it % 2] = t_spl
                        t_comb2[psx - 2] = t_c
                        if psx == 3:
                            ps3g[0] = ps3g[1] = t_c
                        ln_stats(buf, st12, mv)
                        dve.op("tensor_tensor", sc[:, 2:3], mv[:, 0:1], mv[:, 0:1],
                               ALU.mult)
                        dve.op("tensor_tensor", sc[:, 3:4], mv[:, 1:2], sc[:, 2:3],
                               ALU.add)
                        dve.op("tensor_scalar", sc[:, 3:4], sc[:, 3:4], EPS_F,
                               None, ALU.add)
                        t_rr = dve.op("reciprocal", sc[:, 3:4], sc[:, 3:4])
                        act.wait(t_rr)
                        t_sq = act.op("sqrt", sc[:, 3:4], sc[:, 3:4])
                        dve.wait(t_sq)
                        dve.wait(S.get("h2f_guard"))
                        for hh in range(2):
                            dve.op("tensor_scalar", h2f[:, hh * 512:(hh + 1) * 512],
                                   buf[:, hh * 512:(hh + 1) * 512], sc[:, 3:4],
                                   None, ALU.mult)
                            t_h2 = dve.op("tensor_tensor",
                                          h2f[:, hh * 512:(hh + 1) * 512],
                                          h2f[:, hh * 512:(hh + 1) * 512],
                                          ln2_bc[:, hh * 512:(hh + 1) * 512],
                                          ALU.mult)
                        pe.wait(t_h2)
                        pe.wait(S.get("rt_ps_guard"))
                        pe.wait(pool_g[0])
                        pe.wait(pool_g[1])
                        for c in range(NCH):
                            pst = PS[0] if c < 4 else PS[1]
                            t_ht = pe.op("transpose",
                                         pst[:, (c % 4) * 128:(c % 4 + 1) * 128],
                                         h2f[:, c * 128:(c + 1) * 128], identf[:])
                        dve.wait(t_ht)
                        dve.op("tensor_copy", h2T[:, 0:512], PS[0][:, 0:512])
                        t_he = dve.op("tensor_copy", h2T[:, 512:1024],
                                      PS[1][:, 0:512])
                        S["h2f_guard"] = t_he
                        pool_g[0] = pool_g[1] = t_he
                        pe.wait(t_he)
                        for c in range(NCH):
                            t_lg = pe.op("matmul", PS[0][0:8, 0:128],
                                         wr_sb[:, c * 8:(c + 1) * 8],
                                         h2T[:, c * 128:(c + 1) * 128],
                                         start=(c == 0), stop=(c == NCH - 1))
                        dve.wait(t_lg)
                        t_le = dve.op("tensor_copy", lgs, PS[0][0:8, 0:128])
                        pe.wait(t_le)
                        t_lt = pe.op("transpose", PS[1][:, 0:8], lgs,
                                     identf[0:8, 0:8])
                        dve.wait(t_lt)
                        dve.op("tensor_tensor", m8[:, 0:8], PS[1][:, 0:8],
                               br_bc[:], ALU.add)
                        dve.op("reduce_max", sc[:, 4:5], m8[:, 0:8], AX.X)
                        dve.op("tensor_scalar", m8[:, 8:16], m8[:, 0:8],
                               sc[:, 4:5], None, ALU.is_ge)
                        dve.op("tensor_scalar", m8[:, 16:24], m8[:, 8:16],
                               1e30, 1e30, ALU.mult, ALU.subtract)
                        mkb = m8[:, 0:8].bitcast(BF16)
                        t_mk = dve.op("tensor_copy", mkb[:, 0:8], m8[:, 8:16])
                        t_pn = dve.op("tensor_copy", mkb[:, 8:16], m8[:, 16:24])
                        pe.wait(t_pn)
                        t_mt1 = pe.op("transpose",
                                      PS[1][0:8, 128:192].bitcast(BF16),
                                      mkb[:, 0:8], identb[:])
                        t_mt2 = pe.op("transpose",
                                      PS[1][0:8, 192:256].bitcast(BF16),
                                      mkb[:, 8:16], identb[:])
                        dve.wait(t_mt1)
                        dve.wait(t_mt2)
                        dve.op("tensor_copy", mask_fm[:, i * 128:(i + 1) * 128],
                               PS[1][0:8, 128:192].bitcast(BF16))
                        t_pe2 = dve.op("tensor_copy",
                                       pen_fm[:, i * 128:(i + 1) * 128],
                                       PS[1][0:8, 192:256].bitcast(BF16))
                        S["rt_ps_guard"] = t_pe2
                        pool_g[0] = pool_g[1] = t_pe2
            P.tap(f"Ogrp_{blk}", O_grp[:], dve.tok())
            P.tap(f"z_{blk}", z_sb[:], dve.tok())

        def phase4(s):
            wqf = arb(O_W, 8192)
            wkf = arb(O_W + 4096, 2048)
            wvf = arb(O_W + 5120, 2048)
            t_wq = pl.op("w0", "dma_start",
                         out=wqf.rearrange("p (c n) -> p c n", n=DM),
                         in_=wfq_d.rearrange("(c p) n -> p c n", p=128))
            t_wk = pl.op("w1", "dma_start",
                         out=wkf.rearrange("p (c n) -> p c n", n=256),
                         in_=wfk_d.rearrange("(c p) n -> p c n", p=128))
            t_wv = pl.op("wg", "dma_start",
                         out=wvf.rearrange("p (c n) -> p c n", n=256),
                         in_=wfv_d.rearrange("(c p) n -> p c n", p=128))
            t_wo = pl.op("w3", "dma_start",
                         out=arb(O_W + 8192, 8192).rearrange(
                             "p (c n) -> p c n", n=DM),
                         in_=wfo_d.rearrange("(c p) n -> p c n", p=128))
            P.pe.wait(t_wo)
            x1b = [arf(O_SCR, 1024), arf(O_SCR + 1024, 1024)]
            tmp = arf(O_SCR + 2048, 512)
            hbf = arb(O_SCR + 2560, 1024)
            hfm_t = arb(O_SCR + 3072, 1024)
            qtok = arf(O_SCR + 3584, 1024)
            qrope = arf(O_SCR + 4608, 1024)
            qbf = arb(O_SCR + 5632, 1024)
            ktok = arf(O_SCR + 6144, 256)
            krope = arf(O_SCR + 6400, 256)
            kbf = arb(O_SCR + 6656, 256)
            st12 = arf(O_SCR + 6784, 12)
            mv = arf(O_SCR + 6796, 2)
            sc = arf(O_SCR + 6798, 6)
            q_pk = arb(O_QPK, 16384)
            kF = arb(O_KPK, 8192)
            vaugF = arb(O_KPK + 4096, 16 * NKV * 65).rearrange(
                "p (j h d) -> p j h d", h=NKV, d=65)

            def rope(src_f, dst_f, nh, i):
                sv = src_f.rearrange("p (h d) -> p h d", d=HD)
                dv = dst_f.rearrange("p (h d) -> p h d", d=HD)
                tv = tmp.rearrange("p (h d) -> p h d", d=32)[:, 0:nh, :]
                cosv = cos_t[:, i * 32:(i + 1) * 32].rearrange(
                    "p d -> p () d").broadcast_to((128, nh, 32))
                sinv = sin_t[:, i * 32:(i + 1) * 32].rearrange(
                    "p d -> p () d").broadcast_to((128, nh, 32))
                x1v, x2v = sv[:, :, 0:32], sv[:, :, 32:64]
                dve.op("tensor_tensor", dv[:, :, 0:32], x1v, cosv, ALU.mult)
                dve.op("tensor_tensor", tv, x2v, sinv, ALU.mult)
                dve.op("tensor_tensor", dv[:, :, 0:32], dv[:, :, 0:32], tv,
                       ALU.subtract)
                dve.op("tensor_tensor", dv[:, :, 32:64], x1v, sinv, ALU.mult)
                dve.op("tensor_tensor", tv, x2v, cosv, ALU.mult)
                return dve.op("tensor_tensor", dv[:, :, 32:64], dv[:, :, 32:64],
                              tv, ALU.add)

            def qknorm(rsrc, dst_bf, sq_buf, nh, wn_bc, scale2):
                rv = rsrc.rearrange("p (h d) -> p h d", d=HD)
                dve.op("tensor_tensor", sq_buf[:, 0:nh * HD], rsrc, rsrc, ALU.mult)
                dve.op("tensor_reduce", sc[:, 0:nh] if nh <= 6 else ms16,
                       sq_buf[:, 0:nh * HD].rearrange("p (h d) -> p h d", d=HD),
                       AX.X, ALU.add)
                msx = sc[:, 0:nh] if nh <= 6 else ms16
                dve.op("tensor_scalar", msx, msx, 1.0 / HD, EPS_F,
                       ALU.mult, ALU.add)
                t = dve.op("reciprocal", msx, msx)
                act.wait(t)
                t_s = act.op("activation", msx, msx, AF.Sqrt, scale=float(scale2))
                dve.wait(t_s)
                dve.op("tensor_tensor", sq_buf[:, 0:nh * HD].rearrange(
                    "p (h d) -> p h d", d=HD), rv,
                    msx.rearrange("p h -> p h ()").broadcast_to((128, nh, HD)),
                    ALU.mult)
                return dve.op(
                    "tensor_tensor", dst_bf.rearrange("p (h d) -> p h d", d=HD),
                    sq_buf[:, 0:nh * HD].rearrange("p (h d) -> p h d", d=HD),
                    wn_bc.rearrange("p d -> p () d").broadcast_to((128, nh, HD)),
                    ALU.mult)

            ms16 = arf(O_SCR + 6804, 16)
            x1_use = [None, None]
            hfm_guard = None
            qk_ps = [None, None]
            v_ps = None
            tr_ev = [None, None]
            trq_ev = [None, None]
            qbf_guard = None
            kbf_guard = None
            for i in range(NT):
                buf = x1b[i % 2]
                sp.wait(x1_use[i % 2])
                t_x = sp.op(f"x0{i % 2}", "dma_start", out=buf,
                            in_=x1_sp[s, i * 128:(i + 1) * 128, :])
                dve.wait(t_x)
                ln_stats(buf, st12, mv)
                t_c = entry_norm(buf, mv, sc, EPS_F)
                dve.wait(hfm_guard)
                dve.wait(S.get("p4_hfm_pe"))
                for hh in range(2):
                    dve.op("tensor_scalar", tmp, buf[:, hh * 512:(hh + 1) * 512],
                           mv[:, 0:1], sc[:, 0:1], ALU.subtract, ALU.mult)
                    t_h = dve.op("tensor_tensor",
                                 hbf[:, hh * 512:(hh + 1) * 512], tmp,
                                 ln1_bc[:, hh * 512:(hh + 1) * 512], ALU.mult)
                x1_use[i % 2] = t_h
                for c in range(NCH):
                    slot = c % 2
                    trp = (PS[3][:, 0:64] if slot == 0
                           else PS[3][:, 512:576]).bitcast(BF16)
                    pe.wait(t_h)
                    pe.wait(tr_ev[slot])
                    t_t = pe.op("transpose", trp, hbf[:, c * 128:(c + 1) * 128],
                                identb[:])
                    dve.wait(t_t)
                    tr_ev[slot] = dve.op(
                        "tensor_copy", hfm_t[:, c * 128:(c + 1) * 128], trp)
                hfm_guard = t_t
                pe.wait(tr_ev[0])
                pe.wait(tr_ev[1])
                pe.wait(t_wq)
                pe.wait(t_wk)
                pe.wait(t_wv)
                for dg in range(2):
                    ps = PS[0][:, dg * 512:(dg + 1) * 512]
                    pe.wait(qk_ps[dg])
                    pe.op("matmul", ps, ones_bf[0:1, 0:128],
                          bq_bf[0:1, dg * 512:(dg + 1) * 512],
                          start=True, stop=False)
                    for c in range(NCH):
                        t_mm = pe.op("matmul", ps, hfm_t[:, c * 128:(c + 1) * 128],
                                     wqf[:, c * DM + dg * 512:
                                         c * DM + (dg + 1) * 512],
                                     start=False, stop=(c == NCH - 1))
                    dve.wait(t_mm)
                    qk_ps[dg] = dve.op("tensor_copy",
                                       qtok[:, dg * 512:(dg + 1) * 512], ps)
                psk = PS[1][:, 0:256]
                pe.wait(S.get("p4_kps"))
                pe.op("matmul", psk, ones_bf[0:1, 0:128], bk_bf[0:1, :],
                      start=True, stop=False)
                for c in range(NCH):
                    t_mm = pe.op("matmul", psk, hfm_t[:, c * 128:(c + 1) * 128],
                                 wkf[:, c * 256:(c + 1) * 256],
                                 start=False, stop=(c == NCH - 1))
                dve.wait(t_mm)
                S["p4_kps"] = dve.op("tensor_copy", ktok, psk)
                psv = PS[1][:, 512:768]
                pe.wait(v_ps)
                pe.op("matmul", psv, ones_bf[0:1, 0:128], bv_bf[0:1, :],
                      start=True, stop=False)
                for c in range(NCH):
                    t_mm = pe.op("matmul", psv, hfm_t[:, c * 128:(c + 1) * 128],
                                 wvf[:, c * 256:(c + 1) * 256],
                                 start=False, stop=(c == NCH - 1))
                dve.wait(t_mm)
                v_ps = dve.op("tensor_copy", vaugF[:, i, :, 0:64],
                              psv.rearrange("p (h d) -> p h d", d=HD))
                S["p4_hfm_pe"] = t_mm
                rope(qtok, qrope, NH, i)
                dve.wait(qbf_guard)
                t_qb = qknorm(qrope, qbf, qtok, NH, qn_bc, 0.125 * 0.125)
                rope(ktok, krope, NKV, i)
                dve.wait(kbf_guard)
                t_kb = qknorm(krope, kbf, ktok, NKV, kn_bc, 1.0)
                for jp in range(NH // 2):
                    slot = jp % 2
                    trp = (PS[2][:, 0:64] if slot == 0
                           else PS[2][:, 512:576]).bitcast(BF16)
                    pe.wait(t_qb)
                    pe.wait(trq_ev[slot])
                    pe.op("transpose", trp[0:64, :],
                          qbf[:, (2 * jp) * HD:(2 * jp + 1) * HD], identb[:])
                    t_t = pe.op("transpose", trp[64:128, :],
                                qbf[:, (2 * jp + 1) * HD:(2 * jp + 2) * HD],
                                identb[:], tile_position=(0, 64))
                    dve.wait(t_t)
                    trq_ev[slot] = dve.op(
                        "tensor_copy",
                        q_pk[:, jp * 2048 + i * 128: jp * 2048 + (i + 1) * 128],
                        trp)
                qbf_guard = t_t
                for kh in range(NKV):
                    slot = kh % 2
                    trp = (PS[2][:, 0:64] if slot == 0
                           else PS[2][:, 512:576]).bitcast(BF16)
                    pe.wait(t_kb)
                    pe.wait(trq_ev[slot])
                    pe.op("transpose", trp[0:64, :],
                          kbf[:, kh * HD:(kh + 1) * HD], identb[:])
                    t_t = pe.op("transpose", trp[64:128, :],
                                kbf[:, kh * HD:(kh + 1) * HD], identb[:],
                                tile_position=(0, 64))
                    dve.wait(t_t)
                    trq_ev[slot] = dve.op(
                        "tensor_copy",
                        kF[:, kh * 2048 + i * 128: kh * 2048 + (i + 1) * 128],
                        trp)
                kbf_guard = t_t
            dve.wait(v_ps)
            dve.op("memset", vaugF[:, :, :, HD:], 1.0)
            P.tap("qF_pk", q_pk[:], dve.tok())
            P.tap("kF_pk", kF[:], dve.tok())
            P.tap("vF_aug", arb(O_KPK + 4096, 16 * NKV * 65)[:], dve.tok())

        def phase7(s):
            h2fm = arb(O_QPK, 8192)
            w1a = [arb(O_QPK + 4096, 4096), arb(O_QPK + 6144, 4096)]
            w1b = [arb(O_VAG, 4096), arb(O_VAG + 2048, 4096)]
            w2sb2 = [arb(O_KPK, 16384), arb(O_W, 16384)]
            actb = arb(O_VAG + 4096, 8192)
            y_q = arf(O_W + 8192, 4096)
            pen = [arb(O_SCR, 512, p=1), arb(O_SCR + 256, 512, p=1)]
            sg = [arf(O_SCR + 512, 512), arf(O_SCR + 1024, 512),
                  arf(O_SCR + 1536, 512), arf(O_SCR + 2048, 512)]
            xinb = arf(O_SCR + 2560, 1024)
            x1b = arf(O_SCR + 3584, 1024)
            h2bf = arb(O_SCR + 4608, 1024)
            tmp = arf(O_SCR + 5120, 512)
            st12 = arf(O_SCR + 5632, 12)
            mv = arf(O_SCR + 5644, 2)
            sc = arf(O_SCR + 5646, 6)

            xin_use = None
            x1_use = None
            h2_guard = None
            tr_ev = [None, None]
            u1g = [None, None, None, None]
            u2g = [None, None, None, None]
            sg_use = [None, None, None, None]
            act_guard = {}
            y_ps = [None, None]
            yq_guard = [None, None]
            out_guard2 = [None, None]
            pen_use = [None, None]
            b1_use = [None, None]
            w2_use = [None, None]
            wa_use = [None, None]
            wb_use = [None, None]
            for tq in range(4):
                dve.wait(w2_use[0])
                dve.wait(w2_use[1])
                yq_guard = [pe.tok(), pe.tok()]
                for it in range(4):
                    i = 4 * tq + it
                    sp.wait(xin_use)
                    t_x = sp.op("xq", "dma_start", out=xinb,
                                in_=xin_sp[s, i * 128:(i + 1) * 128, :])
                    dve.wait(t_x)
                    ln_stats(xinb, st12, mv)
                    dve.op("tensor_tensor", sc[:, 2:3], mv[:, 0:1], mv[:, 0:1],
                           ALU.mult)
                    dve.op("tensor_tensor", sc[:, 3:4], mv[:, 1:2], sc[:, 2:3],
                           ALU.add)
                    dve.op("tensor_scalar", sc[:, 3:4], sc[:, 3:4], EPS_F,
                           None, ALU.add)
                    t_rr = dve.op("reciprocal", sc[:, 3:4], sc[:, 3:4])
                    act.wait(t_rr)
                    t_sq = act.op("sqrt", sc[:, 3:4], sc[:, 3:4])
                    dve.wait(t_sq)
                    dve.wait(h2_guard)
                    for hh in range(2):
                        dve.op("tensor_scalar", tmp,
                               xinb[:, hh * 512:(hh + 1) * 512], sc[:, 3:4],
                               None, ALU.mult)
                        t_h2 = dve.op("tensor_tensor",
                                      h2bf[:, hh * 512:(hh + 1) * 512], tmp,
                                      ln2_bc[:, hh * 512:(hh + 1) * 512],
                                      ALU.mult)
                    xin_use = t_h2
                    for c in range(NCH):
                        slot = c % 2
                        trp = (PS[3][:, 0:64] if slot == 0
                               else PS[3][:, 512:576]).bitcast(BF16)
                        pe.wait(t_h2)
                        pe.wait(tr_ev[slot])
                        pe.wait(u2g[2 + slot])
                        t_t = pe.op("transpose", trp,
                                    h2bf[:, c * 128:(c + 1) * 128], identb[:])
                        dve.wait(t_t)
                        tr_ev[slot] = dve.op(
                            "tensor_copy",
                            h2fm[:, c * 512 + it * 128: c * 512 + (it + 1) * 128],
                            trp)
                    h2_guard = t_t
                for e in range(8):
                    es = e % 2
                    sp.wait(pen_use[es])
                    t_pen = sp.op(f"pen{es}", "dma_start", out=pen[es][0:1, :],
                                  in_=pen_fm[e:e + 1, tq * 512:(tq + 1) * 512])
                    sp.wait(b1_use[es])
                    t_b1 = sp.op(f"b1{es}", "dma_start", out=b1c[es][:],
                                 in_=b1c_d[e])
                    w2sb = w2sb2[es]
                    pl.wait(w2_use[es])
                    t_w2 = pl.op(f"w2s{es}", "dma_start",
                                 out=w2sb.rearrange("p (c n) -> p c n", n=DM),
                                 in_=w2_d[e].rearrange("(c p) n -> p c n", p=128))
                    for gp in range(4):
                        ws = gp % 2
                        pl.wait(wa_use[ws])
                        t_wa = pl.op(f"wa{ws}", "dma_start",
                                     out=w1a[ws].rearrange("p (c n) -> p c n",
                                                           n=512),
                                     in_=w1_d[e][:, gp * 512:(gp + 1) * 512]
                                     .rearrange("(c p) n -> p c n", p=128))
                        pl.wait(wb_use[ws])
                        t_wb = pl.op(f"wb{ws}", "dma_start",
                                     out=w1b[ws].rearrange("p (c n) -> p c n",
                                                           n=512),
                                     in_=w1_d[e][:, HID + gp * 512:
                                                 HID + (gp + 1) * 512]
                                     .rearrange("(c p) n -> p c n", p=128))
                        for fc in range(4):
                            hc = gp * 4 + fc
                            ps1 = (PS[0] if fc < 2 else PS[2])[
                                :, (fc % 2) * 512:(fc % 2) * 512 + 512]
                            pe.wait(u1g[fc])
                            pe.wait(t_wa)
                            pe.wait(tr_ev[0])
                            pe.wait(tr_ev[1])
                            for c in range(NCH):
                                t_u1 = pe.op("matmul", ps1,
                                             w1a[ws][:, c * 512 + fc * 128:
                                                     c * 512 + (fc + 1) * 128],
                                             h2fm[:, c * 512:(c + 1) * 512],
                                             start=(c == 0), stop=(c == NCH - 1))
                            ps2 = (PS[1] if fc < 2 else PS[3])[
                                :, (fc % 2) * 512:(fc % 2) * 512 + 512]
                            pe.wait(u2g[fc])
                            pe.wait(t_wb)
                            pe.wait(t_pen)
                            pe.op("matmul", ps2, ones_bf[0:1, 0:128],
                                  pen[es][0:1, :], start=True, stop=False)
                            for c in range(NCH):
                                t_u2 = pe.op("matmul", ps2,
                                             w1b[ws][:, c * 512 + fc * 128:
                                                     c * 512 + (fc + 1) * 128],
                                             h2fm[:, c * 512:(c + 1) * 512],
                                             start=False, stop=(c == NCH - 1))
                            act.wait(t_u2)
                            act.wait(t_b1)
                            act.wait(sg_use[fc])
                            t_sg = act.op("activation", sg[fc], ps2,
                                          AF.Sigmoid,
                                          bias=b1c[es][:, 16 + hc:17 + hc])
                            u2g[fc] = t_sg
                            dve.wait(t_sg)
                            dve.wait(t_u1)
                            dve.wait(act_guard.get(hc))
                            t_ac = dve.op("scalar_tensor_tensor",
                                          actb[:, hc * 512:(hc + 1) * 512],
                                          ps1, b1c[es][:, hc:hc + 1], sg[fc],
                                          ALU.add, ALU.mult)
                            u1g[fc] = t_ac
                            sg_use[fc] = t_ac
                            act_guard[hc] = t_ac
                        wa_use[ws] = t_u1
                        wb_use[ws] = t_u2
                    pe.wait(t_w2)
                    pe.wait(u1g[2])
                    pe.wait(u1g[3])
                    for mc in range(NCH):
                        ysl = mc % 2
                        ps = PS[2][:, ysl * 512:(ysl + 1) * 512]
                        pe.wait(y_ps[ysl])
                        first = True
                        if e == 0:
                            pe.op("matmul", ps, b2_bf[:, mc * 128:(mc + 1) * 128],
                                  mask_fm[:, tq * 512:(tq + 1) * 512],
                                  start=True, stop=False)
                            first = False
                        for hc in range(16):
                            pe.wait(act_guard[hc])
                            t_y = pe.op("matmul", ps,
                                        w2sb[:, hc * 1024 + mc * 128:
                                             hc * 1024 + (mc + 1) * 128],
                                        actb[:, hc * 512:(hc + 1) * 512],
                                        start=first, stop=(hc == 15))
                            first = False
                        dve.wait(t_y)
                        dve.wait(yq_guard[ysl])
                        if e == 0:
                            t_yq = dve.op("tensor_copy",
                                          y_q[:, mc * 512:(mc + 1) * 512], ps)
                        else:
                            t_yq = dve.op("tensor_tensor",
                                          y_q[:, mc * 512:(mc + 1) * 512], ps,
                                          y_q[:, mc * 512:(mc + 1) * 512],
                                          ALU.add)
                        y_ps[ysl] = t_yq
                        yq_guard[ysl] = t_yq
                        u1g[2 + ysl] = t_yq
                    pen_use[es] = t_u2
                    b1_use[es] = t_ac
                    w2_use[es] = t_y
                    for hc in range(16):
                        act_guard[hc] = t_y
                for it in range(4):
                    i = 4 * tq + it
                    psx = 2 if it % 2 == 0 else 3
                    pe.wait(out_guard2[psx - 2])
                    pe.wait(y_ps[0])
                    pe.wait(y_ps[1])
                    if psx == 3:
                        pe.wait(u2g[2])
                        pe.wait(u2g[3])
                    t_tr = None
                    for c in range(NCH):
                        pe.wait(yq_guard[c % 2])
                        t_tr = pe.op("transpose", PS[psx][:, c * 128:(c + 1) * 128],
                                     y_q[:, c * 512 + it * 128:
                                          c * 512 + (it + 1) * 128], identf[:])
                    sp.wait(xin_use)
                    t_xi = sp.op("xq", "dma_start", out=xinb,
                                 in_=xin_sp[s, i * 128:(i + 1) * 128, :])
                    sp.wait(x1_use)
                    t_x1 = sp.op("x1q", "dma_start", out=x1b,
                                 in_=x1_sp[s, i * 128:(i + 1) * 128, :])
                    dve.wait(t_tr)
                    dve.wait(t_xi)
                    for hh in range(2):
                        t_s = dve.op("tensor_tensor",
                                     xinb[:, hh * 512:(hh + 1) * 512],
                                     PS[psx][:, hh * 512:(hh + 1) * 512],
                                     xinb[:, hh * 512:(hh + 1) * 512], ALU.add)
                    out_guard2[psx - 2] = t_s
                    if psx == 3:
                        u2g[2] = u2g[3] = t_s
                    ln_stats(xinb, st12, mv)
                    t_rs = rstd_of(mv[:, 1:2], sc[:, 0:1], EPS_LN)
                    dve.wait(t_rs)
                    dve.wait(t_x1)
                    for hh in range(2):
                        dve.op("tensor_scalar", tmp,
                               xinb[:, hh * 512:(hh + 1) * 512],
                               mv[:, 0:1], sc[:, 0:1], ALU.subtract, ALU.mult)
                        t_o = dve.op("tensor_tensor",
                                     x1b[:, hh * 512:(hh + 1) * 512], tmp,
                                     x1b[:, hh * 512:(hh + 1) * 512], ALU.add)
                    xin_use = t_o
                    sp.wait(t_o)
                    x1_use = sp.op("out", "dma_start",
                                   out=out_d[s, i * 128:(i + 1) * 128, :], in_=x1b)

        for s in range(nseq):
            if nphases >= 1:
                phase1(s)
                P.barrier()
            if nphases >= 2:
                t_wz = pl.op("w0", "dma_start",
                             out=arb(O_W, 8192).rearrange("p (c n) -> p c n", n=DM),
                             in_=wao_d.rearrange("(c p) n -> p c n", p=128))
                P.pe.wait(t_wz)
                attention(s, "A")
                P.barrier()
            if nphases >= 3:
                phase4(s)
                P.barrier()
            if nphases >= 4:
                attention(s, "F")
                P.barrier()
            if nphases >= 5:
                phase7(s)
                P.barrier()

        P.barrier()
        final_sp = P.sp.toks() + P.pl.toks()

        with nc.Block() as block:
            @block.tensor
            def _(t):
                for f in P.streams["pe"]:
                    f(t)

            @block.vector
            def _(v):
                for f in P.streams["dve"]:
                    f(v)

            @block.scalar
            def _(sc_):
                for f in P.streams["act"]:
                    f(sc_)

            @block.sync
            def _(sy):
                for f in P.streams["sp"]:
                    f(sy)
                for key, val in final_sp:
                    sy.wait_ge(P.sems[key], val)

            @block.gpsimd
            def _(gp):
                for f in P.streams["pl"]:
                    f(gp)

    return nc, P


def make_inputs(core_inputs, core_id):
    d = core_inputs
    m = {}
    m["x"] = np.ascontiguousarray(
        np.asarray(d["L0_attn_tokens"], np.float32)[2 * core_id:2 * core_id + 2])
    for k in ("L0_attn_Wq", "L0_attn_Wkv", "L0_attn_Wout", "L0_attn_Wg",
              "L0_ffn_Wq", "L0_ffn_Wk", "L0_ffn_Wv", "L0_ffn_Wg", "L0_ffn_Wo",
              "L0_ffn_W1", "L0_ffn_W2", "L0_ffn_b2"):
        m[k] = np.ascontiguousarray(np.asarray(d[k], dtype=np.float32))
    m["d_normw_bc"] = np.ascontiguousarray(
        np.broadcast_to(np.asarray(d["L0_attn_norm_w"], np.float32), (128, DM)))
    m["d_ln1_bc"] = np.ascontiguousarray(
        np.broadcast_to(np.asarray(d["L0_ffn_ln1_w"], np.float32), (128, DM)))
    m["d_ln2_bc"] = np.ascontiguousarray(
        np.broadcast_to(np.asarray(d["L0_ffn_ln2_w"], np.float32), (128, DM)))
    m["d_qn_bc"] = np.ascontiguousarray(
        np.broadcast_to(np.asarray(d["L0_ffn_qn_w"], np.float32), (128, HD)))
    m["d_kn_bc"] = np.ascontiguousarray(
        np.broadcast_to(np.asarray(d["L0_ffn_kn_w"], np.float32), (128, HD)))
    m["d_br_bc"] = np.ascontiguousarray(
        np.broadcast_to(np.asarray(d["L0_ffn_br"], np.float32), (128, NE)))
    m["d_bo_col"] = np.ascontiguousarray(
        np.asarray(d["L0_ffn_bo"], np.float32).reshape(NCH, 128).T)
    m["d_bg_col"] = np.ascontiguousarray(
        np.asarray(d["L0_ffn_bg"], np.float32).reshape(HD, 1))
    m["d_b1_cols"] = np.ascontiguousarray(
        np.asarray(d["L0_ffn_b1"], np.float32).reshape(NE, 32, 128)
        .transpose(0, 2, 1))
    m["d_wr_cols"] = np.ascontiguousarray(
        np.asarray(d["L0_ffn_Wr"], np.float32).reshape(NCH, 128, NE)
        .transpose(1, 0, 2).reshape(128, 8 * NE))
    m["d_bq_row"] = np.asarray(d["L0_ffn_bq"], np.float32).reshape(1, DM).copy()
    m["d_bk_row"] = np.asarray(d["L0_ffn_bk"], np.float32).reshape(1, 256).copy()
    m["d_bv_row"] = np.asarray(d["L0_ffn_bv"], np.float32).reshape(1, 256).copy()
    m["d_identf"] = np.eye(128, dtype=np.float32)
    m["d_identb"] = np.eye(128, dtype=BFNP)
    jj, qq = np.meshgrid(np.arange(128), np.arange(128), indexing="ij")
    m["d_ulo"] = np.where(jj > qq, np.float32(NEGBIG), np.float32(0)).astype(BFNP)
    inv = (1.0 / (10000.0 ** (np.arange(0, HD, 2, dtype=np.float32)
                              / np.float32(HD)))).astype(np.float32)
    fr = np.outer(np.arange(T, dtype=np.float32), inv).astype(np.float32)
    m["d_cos_t"] = np.ascontiguousarray(
        np.cos(fr).astype(np.float32).reshape(NT, 128, 32)
        .transpose(1, 0, 2).reshape(128, NT * 32))
    m["d_sin_t"] = np.ascontiguousarray(
        np.sin(fr).astype(np.float32).reshape(NT, 128, 32)
        .transpose(1, 0, 2).reshape(128, NT * 32))
    return m


def kernel(**inputs):
    nc, P = build_program()
    in_maps = [make_inputs(inputs, c) for c in range(8)]
    res = run_bass_kernel_spmd(nc, in_maps, core_ids=list(range(8)))
    out = np.concatenate([res.results[c]["out"] for c in range(8)], axis=0)
    return np.ascontiguousarray(out.reshape(16, T, DM)).astype(np.float32)


# version 10 (final)
# speedup vs baseline: 1.7271x; 1.0013x over previous
import contextlib
import numpy as np
import ml_dtypes
import concourse.bass as bass
import concourse.mybir as mybir
from concourse.bass_utils import run_bass_kernel_spmd
from concourse.alu_op_type import AluOpType as ALU

F32 = mybir.dt.float32
BF16 = mybir.dt.bfloat16
AF = mybir.ActivationFunctionType
AX = mybir.AxisListType
BFNP = ml_dtypes.bfloat16

T = 2048
NT = 16
DM = 1024
NCH = 8
NH = 16
HD = 64
NKV = 4
HID = 2048
NE = 8
NG = 4
EPS_LN = 1e-5
EPS_A = float(np.finfo(np.float32).eps)
EPS_F = 1e-6
NEGBIG = -1e30
NSEQ = 2

AR_COLS = 44032
O_QPK = 0
O_KPK = 8192
O_VAG = 16384
O_W = 24704
O_SCR = 36992


class Eng:
    def __init__(self, prog, key, selfchain=False, dma=False):
        self.prog = prog
        self.key = key
        self.selfchain = selfchain
        self.dma = dma
        self.n = 0
        self.waited = {}

    def wait(self, tok):
        if tok is None:
            return
        key, val = tok
        if val <= 0 or self.waited.get(key, 0) >= val:
            return
        self.waited[key] = val
        sems = self.prog.sems

        def thunk(g, key=key, val=val):
            g.wait_ge(sems[key], val)
        self.prog.streams[self.key].append(thunk)

    def op(self, meth, *a, **kw):
        if self.selfchain and self.n > 0:
            self.wait((self.key, self.n))
        d = 16 if self.dma else 1
        self.n += d
        sems = self.prog.sems

        def thunk(g, meth=meth, a=a, kw=kw, key=self.key, d=d):
            getattr(g, meth)(*a, **kw).then_inc(sems[key], d)
        self.prog.streams[self.key].append(thunk)
        return (self.key, self.n)

    def tok(self):
        return (self.key, self.n)


class DmaQ:

    def __init__(self, prog, qkey):
        self.prog = prog
        self.qkey = qkey
        self.chan = {}

    def wait(self, tok):
        if tok is None:
            return
        key, val = tok
        if val <= 0:
            return
        wdict = self.chan.setdefault("_waited", {})
        if isinstance(wdict, dict) and wdict.get(key, 0) >= val:
            return
        wdict[key] = val
        sems = self.prog.sems

        def thunk(g, key=key, val=val):
            g.wait_ge(sems[key], val)
        self.prog.streams[self.qkey].append(thunk)

    def op(self, chan, meth, *a, **kw):
        key = f"{self.qkey}.{chan}"
        if key not in self.prog.sems:
            self.prog.newsem(key)
            self.chan[key] = 0
        self.chan[key] += 16
        n = self.chan[key]
        sems = self.prog.sems

        def thunk(g, meth=meth, a=a, kw=kw, key=key):
            getattr(g, meth)(*a, **kw).then_inc(sems[key], 16)
        self.prog.streams[self.qkey].append(thunk)
        return (key, n)

    def toks(self):
        return [(k, v) for k, v in self.chan.items() if k != "_waited"]


class Prog:
    def __init__(self, nc, dbg=()):
        self.nc = nc
        self.streams = {k: [] for k in ("pe", "dve", "act", "sp", "pl")}
        self.sems = {}
        self.newsem = None
        self.dbg = set(dbg)
        self.dbg_tensors = {}
        self.pe = Eng(self, "pe")
        self.dve = Eng(self, "dve", selfchain=True)
        self.act = Eng(self, "act", selfchain=True)
        self.sp = DmaQ(self, "sp")
        self.pl = DmaQ(self, "pl")

    def barrier(self):
        engs = (self.pe, self.dve, self.act)
        toks = [e.tok() for e in engs] + self.sp.toks() + self.pl.toks()
        for e in engs:
            for t in toks:
                if t[0] != e.key:
                    e.wait(t)
        for q in (self.sp, self.pl):
            for t in toks:
                q.wait(t)

    def tap(self, name, ap, producer_tok):
        if name not in self.dbg or name in self.dbg_tensors:
            return
        t = self.nc.dram_tensor("dbg_" + name, list(ap.shape), ap.dtype,
                                kind="ExternalOutput").ap()
        self.dbg_tensors[name] = t
        self.sp.wait(producer_tok)
        self.sp.op("dbg", "dma_start", out=t, in_=ap)


def build_program(nphases=5, dbg=(), nseq=NSEQ):
    nc = bass.Bass("TRN2", target_bir_lowering=False)
    P = Prog(nc, dbg)

    def din(name, shape, dtype=F32):
        return nc.dram_tensor(name, shape, dtype, kind="ExternalInput").ap()

    x_d = din("x", [NSEQ, T, DM])
    out_d = nc.dram_tensor("out", [NSEQ, T, DM], F32, kind="ExternalOutput").ap()
    x1_sp = nc.dram_tensor("x1_spill", [NSEQ, T, DM], F32, kind="Internal").ap()
    xin_sp = nc.dram_tensor("xin_spill", [NSEQ, T, DM], F32, kind="Internal").ap()

    waq_d = din("L0_attn_Wq", [DM, DM])
    wakv_d = din("L0_attn_Wkv", [DM, 2 * DM])
    wao_d = din("L0_attn_Wout", [DM, DM])
    wag_d = din("L0_attn_Wg", [DM, NH])
    wfq_d = din("L0_ffn_Wq", [DM, DM])
    wfk_d = din("L0_ffn_Wk", [DM, 256])
    wfv_d = din("L0_ffn_Wv", [DM, 256])
    wfg_d = din("L0_ffn_Wg", [HD, HD])
    wfo_d = din("L0_ffn_Wo", [DM, DM])
    w1_d = din("L0_ffn_W1", [NE, DM, 2 * HID])
    w2_d = din("L0_ffn_W2", [NE, HID, DM])
    b2_d = din("L0_ffn_b2", [NE, DM])
    normw_d = din("d_normw_bc", [128, DM])
    ln1_d = din("d_ln1_bc", [128, DM])
    ln2_d = din("d_ln2_bc", [128, DM])
    qn_d = din("d_qn_bc", [128, HD])
    kn_d = din("d_kn_bc", [128, HD])
    br_d = din("d_br_bc", [128, NE])
    bo_d = din("d_bo_col", [128, NCH])
    bgc_d = din("d_bg_col", [HD, 1])
    b1c_d = din("d_b1_cols", [NE, 128, 32])
    wr_d = din("d_wr_cols", [128, 64])
    bq_d = din("d_bq_row", [1, DM])
    bk_d = din("d_bk_row", [1, 256])
    bv_d = din("d_bv_row", [1, 256])
    identf_d = din("d_identf", [128, 128])
    identb_d = din("d_identb", [128, 128], BF16)
    ulo_d = din("d_ulo", [128, 128], BF16)
    cos_d = din("d_cos_t", [128, NT * 32])
    sin_d = din("d_sin_t", [128, NT * 32])

    ctx = contextlib.ExitStack()

    def sbuf(name, shape, dt=F32):
        return ctx.enter_context(nc.sbuf_tensor(name, shape, dt))

    def psum(name, shape, dt=F32):
        return ctx.enter_context(nc.psum_tensor(name, shape, dt))

    with ctx:
        identf = sbuf("identf", [128, 128])
        identb = sbuf("identb", [128, 128], BF16)
        ulo = sbuf("ulo", [128, 128], BF16)
        ones_bf = sbuf("ones_bf", [65, 512], BF16)
        negone_bf = sbuf("negone_bf", [65, 128], BF16)
        ones_f = sbuf("ones_f", [128, 64])
        normw_bc = sbuf("normw_bc", [128, DM])
        ln1_bc = sbuf("ln1_bc", [128, DM])
        ln2_bc = sbuf("ln2_bc", [128, DM])
        qn_bc = sbuf("qn_bc", [128, HD])
        kn_bc = sbuf("kn_bc", [128, HD])
        br_bc = sbuf("br_bc", [128, NE])
        bo_col = sbuf("bo_col", [128, NCH])
        bg_col = sbuf("bg_col", [HD, 1])
        cos_t = sbuf("cos_t", [128, NT * 32], BF16)
        sin_t = sbuf("sin_t", [128, NT * 32], BF16)
        wr_sb = sbuf("wr_sb", [128, 64])
        b2_bf = sbuf("b2_bf", [NE, DM], BF16)
        bq_bf = sbuf("bq_bf", [1, DM], BF16)
        bk_bf = sbuf("bk_bf", [1, 256], BF16)
        bv_bf = sbuf("bv_bf", [1, 256], BF16)
        wgf_bf = sbuf("wgf_bf", [HD, HD], BF16)
        b1c = [sbuf("b1c0", [128, 32]), sbuf("b1c1", [128, 32])]
        mask_fm = sbuf("mask_fm", [NE, T], BF16)
        g_sb_t = sbuf("g_sb", [NH, T], BF16)
        pen_fm = sbuf("pen_fm", [NE, T], BF16)
        AR = sbuf("arena", [128, AR_COLS])
        PS = [psum(f"ps{i}", [128, 1024]) for i in range(4)]

        def arf(off, ncols, p0=0, p=128):
            assert 0 <= off and off + ncols <= AR_COLS, (off, ncols)
            return AR[p0:p0 + p, off:off + ncols]

        def arb(off, nbf, p0=0, p=128):
            assert 0 <= off and off + (nbf + 1) // 2 <= AR_COLS, (off, nbf)
            return AR[p0:p0 + p, off:off + (nbf + 1) // 2].bitcast(BF16)

        def newsem(key):
            P.sems[key] = ctx.enter_context(
                nc.semaphore("sem_" + key.replace(".", "_")))
        P.newsem = newsem
        for k in ("pe", "dve", "act"):
            newsem(k)
        pe, dve, act, sp, pl = P.pe, P.dve, P.act, P.sp, P.pl
        S = {}

        for dst, src in [(identf, identf_d), (identb, identb_d), (ulo, ulo_d),
                         (normw_bc, normw_d), (ln1_bc, ln1_d), (ln2_bc, ln2_d),
                         (qn_bc, qn_d), (kn_bc, kn_d), (br_bc, br_d),
                         (bo_col, bo_d), (bg_col, bgc_d), (wr_sb, wr_d)]:
            sp.op("init", "dma_start", out=dst[:], in_=src)
        for dst, src in [(cos_t, cos_d), (sin_t, sin_d)]:
            pl.op("binit", "dma_start", out=dst[:], in_=src)
        for dst, src in [(b2_bf, b2_d), (bq_bf, bq_d), (bk_bf, bk_d),
                         (bv_bf, bv_d), (wgf_bf, wfg_d)]:
            pl.op("binit", "dma_start", out=dst[:], in_=src)
        dve.op("memset", ones_bf[:], 1.0)
        dve.op("memset", negone_bf[:], -1.0)
        dve.op("memset", ones_f[:], 1.0)
        dve.op("memset",
               arb(O_VAG, 16 * NH * 65).rearrange("p (j o) -> p j o", o=65)[:, :, HD:],
               1.0)
        P.barrier()

        def ln_stats(src_ap, st12, mv, ncols=1024):
            nhh = (ncols + 511) // 512
            for hh in range(nhh):
                dve.op("bn_stats", st12[:, 6 * hh:6 * hh + 6],
                       src_ap[:, hh * 512:min(ncols, (hh + 1) * 512)])
            return dve.op("bn_aggr", mv, st12[:, 0:6 * nhh])

        def rstd_of(var_ap, out_ap, eps, scale2=1.0):
            dve.op("tensor_scalar", out_ap, var_ap, float(eps), None, ALU.add)
            t = dve.op("reciprocal", out_ap, out_ap)
            act.wait(t)
            if scale2 == 1.0:
                return act.op("sqrt", out_ap, out_ap)
            return act.op("activation", out_ap, out_ap, AF.Sqrt, scale=float(scale2))

        def entry_norm(buf, mv, sc, eps2):
            rstd = sc[:, 0:1]
            c2 = sc[:, 1:2]
            t_r1 = rstd_of(mv[:, 1:2], rstd, EPS_LN)
            dve.wait(t_r1)
            dve.op("tensor_tensor", c2, rstd, rstd, ALU.mult)
            dve.op("tensor_scalar", c2, c2, -EPS_LN, 1.0 + eps2, ALU.mult, ALU.add)
            t = dve.op("reciprocal", c2, c2)
            act.wait(t)
            t_s2 = act.op("sqrt", c2, c2)
            dve.wait(t_s2)
            return dve.op("tensor_tensor", sc[:, 0:1], c2, rstd, ALU.mult)

        def phase1(s):
            wqa = arb(O_W, 8192)
            wkv = arb(O_W + 4096, 16384)
            wga = arb(O_SCR + 6688, 128)
            t_wq = pl.op("w0", "dma_start", out=wqa.rearrange("p (c n) -> p c n", n=DM),
                         in_=waq_d.rearrange("(c p) n -> p c n", p=128))
            t_wkv = pl.op("w1", "dma_start", out=wkv.rearrange("p (c n) -> p c n", n=2 * DM),
                          in_=wakv_d.rearrange("(c p) n -> p c n", p=128))
            t_wg = pl.op("wg", "dma_start", out=wga.rearrange("p (c n) -> p c n", n=NH),
                         in_=wag_d.rearrange("(c p) n -> p c n", p=128))
            x0b = [arf(O_SCR, 1024), arf(O_SCR + 1024, 1024)]
            abf = [arb(O_SCR + 2048, 1024), arb(O_SCR + 2560, 1024)]
            afm_g = arb(O_SCR + 3072, 4096)
            tmp_f = arf(O_SCR + 5120, 512)
            st12 = arf(O_SCR + 5632, 12)
            mv = arf(O_SCR + 5644, 2)
            sc = arf(O_SCR + 5648, 4)
            g_bf = g_sb_t
            q_pk = arb(O_QPK, 16384)
            k_pk = arb(O_KPK, 16384)
            vaug3 = arb(O_VAG, 16 * NH * 65).rearrange("p (j h d) -> p j h d",
                                                       h=NH, d=65)
            dve.op("memset",
                   arb(O_VAG, 16 * NH * 65).rearrange("p (j o) -> p j o",
                                                      o=65)[:, :, HD:], 1.0)
            t_xuse = [None, None]
            abf_guard = [None, None]
            tr_ev = [None, None]
            qk_ev = [None, None]
            v_ev = [None, None]
            t_gev = None
            afm_pe_guard = None
            for g in range(NG):
                for it in range(4):
                    i = 4 * g + it
                    buf = x0b[i % 2]
                    sp.wait(t_xuse[i % 2])
                    t_x0 = sp.op(f"x0{i % 2}", "dma_start", out=buf,
                                 in_=x_d[s, i * 128:(i + 1) * 128, :])
                    dve.wait(t_x0)
                    ln_stats(buf, st12, mv)
                    t_c = entry_norm(buf, mv, sc, EPS_A)
                    dve.wait(abf_guard[i % 2])
                    for hh in range(2):
                        dve.op("tensor_scalar", tmp_f,
                               buf[:, hh * 512:(hh + 1) * 512],
                               mv[:, 0:1], sc[:, 0:1], ALU.subtract, ALU.mult)
                        t_a = dve.op("tensor_tensor",
                                     abf[i % 2][:, hh * 512:(hh + 1) * 512], tmp_f,
                                     normw_bc[:, hh * 512:(hh + 1) * 512], ALU.mult)
                    t_xuse[i % 2] = t_a
                    dve.wait(afm_pe_guard)
                    t_t = None
                    for c in range(NCH):
                        slot = c % 2
                        trp = (PS[3][:, 0:64] if slot == 0
                               else PS[3][:, 512:576]).bitcast(BF16)
                        pe.wait(t_a)
                        pe.wait(tr_ev[slot])
                        t_t = pe.op("transpose", trp,
                                    abf[i % 2][:, c * 128:(c + 1) * 128], identb[:])
                        dve.wait(t_t)
                        tr_ev[slot] = dve.op(
                            "tensor_copy",
                            afm_g[:, c * 512 + it * 128: c * 512 + (it + 1) * 128],
                            trp)
                    abf_guard[i % 2] = t_t
                pe.wait(t_wq)
                pe.wait(t_wkv)
                pe.wait(tr_ev[0])
                pe.wait(tr_ev[1])
                for which in ("q", "k"):
                    wsb, dstpk, stride = ((wqa, q_pk, DM) if which == "q"
                                          else (wkv, k_pk, 2 * DM))
                    for j in range(NH // 2):
                        slot = j % 2
                        ps = PS[0][:, slot * 512:(slot + 1) * 512]
                        pe.wait(qk_ev[slot])
                        for half in range(2):
                            n0 = j * 128 + half * 64
                            tp = (0, 64) if half == 1 else None
                            outap = ps[64:128, :] if half == 1 else ps[0:64, :]
                            for c in range(NCH):
                                t_mm = pe.op(
                                    "matmul", outap,
                                    wsb[:, c * stride + n0: c * stride + n0 + 64],
                                    afm_g[:, c * 512:(c + 1) * 512],
                                    start=(c == 0),
                                    stop=(c == NCH - 1),
                                    tile_position=tp)
                        dst = dstpk[:, j * 2048 + g * 512: j * 2048 + (g + 1) * 512]
                        if which == "q":
                            act.wait(t_mm)
                            qk_ev[slot] = act.op("activation", dst, ps, AF.Copy,
                                                 scale=0.125)
                        else:
                            dve.wait(t_mm)
                            qk_ev[slot] = dve.op("tensor_copy", dst, ps)
                for it in range(4):
                    i = 4 * g + it
                    for dg in range(2):
                        ps = PS[1][:, dg * 512:(dg + 1) * 512]
                        pe.wait(v_ev[dg])
                        for c in range(NCH):
                            t_mm = pe.op("matmul", ps,
                                         afm_g[:, c * 512 + it * 128:
                                               c * 512 + (it + 1) * 128],
                                         wkv[:, c * 2048 + 1024 + dg * 512:
                                             c * 2048 + 1024 + (dg + 1) * 512],
                                         start=(c == 0), stop=(c == NCH - 1))
                        dve.wait(t_mm)
                        v_ev[dg] = dve.op(
                            "tensor_copy", vaug3[:, i, dg * 8:(dg + 1) * 8, 0:64],
                            ps.rearrange("p (h d) -> p h d", d=64))
                pe.wait(t_wg)
                if t_gev is not None:
                    pe.wait(t_gev)
                psg = PS[2][0:16, 0:512]
                for c in range(NCH):
                    t_mm = pe.op("matmul", psg, wga[:, c * 16:(c + 1) * 16],
                                 afm_g[:, c * 512:(c + 1) * 512],
                                 start=(c == 0), stop=(c == NCH - 1))
                act.wait(t_mm)
                t_gev = act.op("activation", g_bf[:, g * 512:(g + 1) * 512], psg,
                               AF.Sigmoid)
                afm_pe_guard = pe.tok()
            S["g_bf"] = g_bf
            S["t_g"] = t_gev
            P.tap("q_pk", q_pk[:], act.tok())
            P.tap("k_pk", k_pk[:], dve.tok())
            P.tap("v_aug", arb(O_VAG, 16 * NH * 65)[:], dve.tok())
            P.tap("g_bf", g_bf[:], act.tok())

        def attention(s, blk):
            q_pk = arb(O_QPK, 16384)
            if blk == "A":
                k_pk = arb(O_KPK, 16384)
                vaug3 = arb(O_VAG, 16 * NH * 65).rearrange(
                    "p (j h d) -> p j h d", h=NH, d=65)
                wz = arb(O_W, 8192)
                O_grp = arb(O_W + 4096, 4096)
                g_bf = S["g_bf"]
            else:
                k_pk = arb(O_KPK, 8192)
                vaug3 = arb(O_KPK + 4096, 16 * NKV * 65).rearrange(
                    "p (j h d) -> p j h d", h=NKV, d=65)
                wz = arb(O_W + 8192, 8192)
                O_grp = arb(O_W, 4096)
            z_sb = arf(O_W + 8192, 4096) if blk == "A" else arf(O_W + 4096, 4096)
            pbuf = [arb(O_SCR, 1024), arb(O_SCR + 512, 1024)]
            rrow = arf(O_SCR + 1024, 512, p=65)
            bcr_sb = arf(O_SCR + 1536, 512, p=64)
            t1_sb = arf(O_SCR + 2048, 512, p=64)
            on_sb = arb(O_SCR + 2560, 512, p=64)
            gate_sb = arf(O_SCR + 2816, 512, p=64)
            grow_b = [arb(O_SCR + 2816, 512, p=1), arb(O_SCR + 3072, 512, p=1),
                      arb(O_SCR + 6472, 512, p=1), arb(O_SCR + 6728, 512, p=1)]
            st12 = arf(O_SCR + 3344, 12)
            mv = arf(O_SCR + 3356, 2)
            sc = arf(O_SCR + 3358, 6)
            q2s = arf(O_SCR + 3368, 512)
            mrow_b = [arb(O_SCR + 3880, 512, p=1), arb(O_SCR + 4136, 512, p=1)]
            km_sc = arf(O_SCR + 4392, 32, p=1)
            x0b = [arf(O_SCR + 4424, 1024), arf(O_SCR + 5448, 1024)]
            h2T = arf(O_VAG, 1024)
            lgs = arf(O_VAG + 1024, 128, p=8)
            m8 = arf(O_VAG + 1160, 24)
            h2f = arf(O_VAG + 1192, 1024)

            pool_g = [None, None]
            pvu = [None, None]
            mrd = [None, None]
            q2s_g = None
            prow_g = None
            g_use = [None, None, None, None]
            gd_tok = [None, None, None, None]
            x0_use = [None, None]
            on_tok = {}
            t_bce = None
            ps2b1_guard = None
            ps3g = [None, None]
            on_guard = None
            t_zdone = [None, None]
            t_comb2 = [None, None]
            t_routr = None
            t_mb_pipe = {}

            def head_geom(h):
                kh = h if blk == "A" else h // 4
                qbase, qrow = (h // 2) * 2048, (h % 2) * 64
                kbase = qbase if blk == "A" else kh * 2048
                return kh, qbase, qrow, kbase, qrow

            def stok_mrow(g, h):
                nonlocal q2s_g, prow_g, t_bce
                kh, qbase, qrow, kbase, krow = head_geom(h)
                mslot = h % 2
                if g == 0 and (blk == "A" or h % 4 == 0):
                    for c4 in range(4):
                        act.wait(q2s_g)
                        t_q2 = act.op("activation", q2s[krow:krow + 64, :],
                                      k_pk[krow:krow + 64,
                                           kbase + c4 * 512: kbase + (c4 + 1) * 512],
                                      AF.Square)
                        pe.wait(t_q2)
                        pe.wait(t_bce)
                        pe.wait(prow_g)
                        t_k2 = pe.op("matmul", PS[2][0:1, 0:512],
                                     ones_f[krow:krow + 64, 0:1],
                                     q2s[krow:krow + 64, :],
                                     start=True, stop=True,
                                     tile_position=(krow, 0) if krow else None)
                        q2s_g = t_k2
                        dve.wait(t_k2)
                        if c4 == 0:
                            t_km = dve.op("reduce_max", km_sc[0:1, 16:17],
                                          PS[2][0:1, 0:512], AX.X)
                        else:
                            dve.op("reduce_max", km_sc[0:1, 17:18],
                                   PS[2][0:1, 0:512], AX.X)
                            t_km = dve.op("tensor_tensor", km_sc[0:1, 16:17],
                                          km_sc[0:1, 16:17], km_sc[0:1, 17:18],
                                          ALU.max)
                        prow_g = t_km
                    t_km = dve.op("tensor_scalar",
                                  km_sc[0:1, (h if blk == "A" else 4 * kh):
                                        (h if blk == "A" else 4 * kh) + 1],
                                  km_sc[0:1, 16:17], 1.0 / HD, None, ALU.mult)
                    S[f"km_{blk}_{kh}"] = t_km
                act.wait(q2s_g)
                t_q2 = act.op("activation", q2s[qrow:qrow + 64, :],
                              q_pk[qrow:qrow + 64,
                                   qbase + g * 512: qbase + (g + 1) * 512],
                              AF.Square)
                pe.wait(t_q2)
                pe.wait(t_bce)
                pe.wait(prow_g)
                t_mm = pe.op("matmul", PS[2][0:1, 0:512],
                             ones_f[qrow:qrow + 64, 0:1],
                             q2s[qrow:qrow + 64, :], start=True, stop=True,
                             tile_position=(qrow, 0) if qrow else None)
                q2s_g = t_mm
                act.wait(t_mm)
                act.wait(mrd[mslot])
                act.wait(S.get(f"km_{blk}_{kh}"))
                kmcol = h if blk == "A" else 4 * kh
                t_mb = act.op("activation", mrow_b[mslot][0:1, :],
                              PS[2][0:1, 0:512], AF.Sqrt,
                              scale=km_sc[0:1, kmcol:kmcol + 1])
                prow_g = t_mb
                if blk == "A":
                    gslot = h % 4
                    sp.wait(S["t_g"])
                    sp.wait(g_use[gslot])
                    gd_tok[gslot] = sp.op(
                        f"gr{gslot}", "dma_start",
                        out=grow_b[gslot][0:1, :],
                        in_=g_bf[h:h + 1, g * 512:(g + 1) * 512])
                return t_mb

            def sprime_pv(g, h, t_mb):
                kh, qbase, qrow, kbase, krow = head_geom(h)
                tp_q = (qrow, 0) if qrow else None
                mslot = h % 2
                njc = 4 * g + 4
                t_pv = None
                t_mm = None
                pend = None
                pairs = [list(range(j0, min(j0 + 2, njc))) for j0 in range(0, njc, 2)]
                for pi, jpair in enumerate(pairs):
                    sl = S.get("pool_slot", 0)
                    S["pool_slot"] = 1 - sl
                    pe.wait(pool_g[sl])
                    pe.wait(t_mb)
                    coffs = []
                    for idx, j in enumerate(jpair):
                        coff = max(0, j * 128 - g * 512)
                        coffs.append(coff)
                        ps = PS[sl][:, idx * 512: idx * 512 + 512]
                        diag = (j * 128 >= g * 512)
                        pe.op("matmul", ps[:, coff:512],
                              k_pk[krow:krow + 64,
                                   kbase + j * 128: kbase + (j + 1) * 128],
                              q_pk[qrow:qrow + 64,
                                   qbase + g * 512 + coff: qbase + (g + 1) * 512],
                              start=True, stop=False, tile_position=tp_q)
                        t_mm = pe.op("matmul", ps[:, coff:512], negone_bf[0:1, :],
                                     mrow_b[mslot][0:1, coff:512],
                                     start=False, stop=not diag)
                        if diag:
                            t_mm = pe.op("matmul", ps[:, coff:coff + 128],
                                         identb[:], ulo[:], start=False, stop=True)
                    act.wait(t_mm)
                    act.wait(pvu[sl])
                    if len(jpair) == 2 and coffs[0] == 0 and coffs[1] == 0:
                        t_exp = act.op("activation", pbuf[sl][:, 0:1024],
                                       PS[sl][:, 0:1024], AF.Exp)
                    else:
                        for idx, j in enumerate(jpair):
                            t_exp = act.op(
                                "activation",
                                pbuf[sl][:, idx * 512 + coffs[idx]:
                                         idx * 512 + 512],
                                PS[sl][:, idx * 512 + coffs[idx]:
                                       idx * 512 + 512], AF.Exp)
                    pool_g[sl] = t_exp

                    ob = (h % 2) * 512

                    def emit_pv(pend):
                        jp, cf, psl, tex = pend
                        pe.wait(tex)
                        if jp[0] == 0:
                            pe.wait(ps3g[h % 2])
                        tpv = None
                        for idx, j in enumerate(jp):
                            tpv = pe.op("matmul", PS[3][0:65, ob + cf[idx]:ob + 512],
                                        vaug3[:, j, kh, :],
                                        pbuf[psl][:, idx * 512 + cf[idx]:
                                                 idx * 512 + 512],
                                        start=(j == 0), stop=(j == njc - 1))
                        pvu[psl] = tpv
                        return tpv
                    if pend is not None:
                        t_pv = emit_pv(pend)
                    pend = (jpair, coffs, sl, t_exp)
                t_pv = emit_pv(pend)
                mrd[mslot] = t_mm
                return t_pv

            def epilogue(g, h, t_pv):
                nonlocal t_bce, ps2b1_guard, on_guard
                kh, qbase, qrow, kbase, krow = head_geom(h)
                mslot = h % 2
                ob = (h % 2) * 512
                dve.wait(t_pv)
                t_rc = dve.op("reciprocal", rrow[64:65, :],
                              PS[3][64:65, ob:ob + 512])
                pe.wait(t_rc)
                pe.wait(t_bce)
                pe.wait(prow_g)
                t_bc = pe.op("matmul", PS[2][0:64, 0:512], ones_f[64:65, 0:64],
                             rrow[64:65, :], start=True, stop=True)
                dve.wait(t_bc)
                t_bce = dve.op("tensor_copy", bcr_sb, PS[2][0:64, 0:512])
                if blk == "A":
                    gslot = h % 4
                    pe.wait(gd_tok[gslot])
                    pe.wait(ps2b1_guard)
                    t_bg = pe.op("matmul", PS[2][0:64, 512:1024],
                                 ones_bf[0:1, 0:64], grow_b[gslot][0:1, :],
                                 start=True, stop=True)
                    dve.wait(on_guard)
                    t_t1 = dve.op("tensor_tensor", t1_sb, PS[3][0:64, ob:ob + 512],
                                  bcr_sb, ALU.mult)
                    ps3g[h % 2] = t_t1
                    dve.wait(t_bg)
                    t_on = dve.op("tensor_tensor", on_sb, t1_sb,
                                  PS[2][0:64, 512:1024], ALU.mult)
                    ps2b1_guard = t_on
                    g_use[gslot] = t_on
                else:
                    dve.wait(on_guard)
                    t_on = dve.op("tensor_tensor", on_sb, PS[3][0:64, ob:ob + 512],
                                  bcr_sb, ALU.mult)
                    ps3g[h % 2] = t_on
                    pe.wait(t_on)
                    pe.wait(ps2b1_guard)
                    t_gl = pe.op("matmul", PS[2][0:64, 512:1024], wgf_bf[:],
                                 on_sb, start=True, stop=True)
                    act.wait(t_gl)
                    t_sg = act.op("activation", gate_sb, PS[2][0:64, 512:1024],
                                  AF.Sigmoid, bias=bg_col[:])
                    ps2b1_guard = t_sg
                    dve.wait(t_sg)
                    t_on = dve.op("tensor_tensor", on_sb, on_sb, gate_sb,
                                  ALU.mult)
                pcol = (h // 2) * 512
                if h % 2 == 0:
                    on_tok[h] = dve.op(
                        "tensor_copy", O_grp[0:64, pcol:pcol + 512], on_sb)
                else:
                    pe.wait(t_on)
                    pe.wait(t_bce)
                    pe.wait(prow_g)
                    t_rl = pe.op("matmul", PS[2][64:128, 0:512],
                                 identb[0:64, 0:64], on_sb,
                                 start=True, stop=True, tile_position=(0, 64))
                    on_guard = t_rl
                    dve.wait(t_rl)
                    on_tok[h] = dve.op(
                        "tensor_copy",
                        O_grp[64:128, pcol:pcol + 512], PS[2][64:128, 0:512])
                    t_bce = on_tok[h]

            pending = stok_mrow(0, 0)
            pend_ep = None
            for g in range(NG):
                for h in range(NH):
                    cur = pending
                    if h + 1 < NH:
                        pending = stok_mrow(g, h + 1)
                    elif g + 1 < NG:
                        pending = stok_mrow(g + 1, 0)
                    t_pv = sprime_pv(g, h, cur)
                    if pend_ep is not None:
                        epilogue(*pend_ep)
                    pend_ep = (g, h, t_pv)
                epilogue(*pend_ep)
                pend_ep = None
                pe.wait(pool_g[0])
                pe.wait(pool_g[1])
                pe.wait(S.get("t_wz_A" if blk == "A" else "t_wz_F"))
                for mchunk in range(NCH):
                    zsl = mchunk % 2
                    ps = PS[zsl][:, 0:512]
                    pe.wait(t_zdone[zsl])
                    for pair in range(NH // 2):
                        pe.wait(on_tok[2 * pair])
                        pe.wait(on_tok[2 * pair + 1])
                        t_mm = pe.op("matmul", ps,
                                     wz[:, pair * 1024 + mchunk * 128:
                                        pair * 1024 + (mchunk + 1) * 128],
                                     O_grp[:, pair * 512:(pair + 1) * 512],
                                     start=(pair == 0), stop=(pair == NH // 2 - 1))
                    if blk == "A":
                        dve.wait(t_mm)
                        t_ze = dve.op("tensor_copy",
                                      z_sb[:, mchunk * 512:(mchunk + 1) * 512], ps)
                    else:
                        act.wait(t_mm)
                        t_ze = act.op("add",
                                      z_sb[:, mchunk * 512:(mchunk + 1) * 512],
                                      ps, bo_col[:, mchunk:mchunk + 1])
                    t_zdone[zsl] = t_ze
                pool_g[0] = t_zdone[0]
                pool_g[1] = t_zdone[1]
                for it in range(4):
                    i = 4 * g + it
                    psx = 2 if it % 2 == 0 else 3
                    zt = PS[psx][:, 0:1024]
                    pe.wait(t_comb2[psx - 2])
                    if psx == 2:
                        pe.wait(t_bce)
                        pe.wait(ps2b1_guard)
                        pe.wait(prow_g)
                    else:
                        pe.wait(ps3g[0])
                        pe.wait(ps3g[1])
                    t_tr = None
                    for c in range(NCH):
                        pe.wait(t_zdone[c % 2])
                        t_tr = pe.op("transpose", PS[psx][:, c * 128:(c + 1) * 128],
                                     z_sb[:, c * 512 + it * 128:
                                          c * 512 + (it + 1) * 128], identf[:])
                    buf = x0b[it % 2]
                    sp.wait(x0_use[it % 2])
                    t_x = sp.op(f"x0{it % 2}", "dma_start", out=buf,
                                in_=(x_d[s, i * 128:(i + 1) * 128, :] if blk == "A"
                                     else x1_sp[s, i * 128:(i + 1) * 128, :]))
                    dve.wait(t_tr)
                    dve.wait(t_x)
                    if blk == "A":
                        ln_stats(zt, st12, mv)
                        t_rs = rstd_of(mv[:, 1:2], sc[:, 0:1], EPS_LN)
                        dve.wait(t_rs)
                        dve.op("tensor_tensor", sc[:, 1:2], mv[:, 0:1],
                               sc[:, 0:1], ALU.mult)
                        for hh in range(2):
                            dve.op("tensor_scalar", buf[:, hh * 512:(hh + 1) * 512],
                                   buf[:, hh * 512:(hh + 1) * 512],
                                   sc[:, 1:2], None, ALU.subtract)
                            t_c = dve.op("scalar_tensor_tensor",
                                         buf[:, hh * 512:(hh + 1) * 512],
                                         zt[:, hh * 512:(hh + 1) * 512],
                                         sc[:, 0:1],
                                         buf[:, hh * 512:(hh + 1) * 512],
                                         ALU.mult, ALU.add)
                        sp.wait(t_c)
                        t_spl = sp.op(f"sx1{it % 2}", "dma_start",
                                      out=x1_sp[s, i * 128:(i + 1) * 128, :],
                                      in_=buf)
                        x0_use[it % 2] = t_spl
                        t_comb2[psx - 2] = t_c
                        if psx == 3:
                            ps3g[0] = ps3g[1] = t_c
                    else:
                        ln_stats(buf, st12, mv)
                        t_rs = rstd_of(mv[:, 1:2], sc[:, 0:1], EPS_LN)
                        dve.wait(t_rs)
                        for hh in range(2):
                            dve.op("tensor_scalar", buf[:, hh * 512:(hh + 1) * 512],
                                   buf[:, hh * 512:(hh + 1) * 512],
                                   mv[:, 0:1], sc[:, 0:1], ALU.subtract, ALU.mult)
                            t_c = dve.op("tensor_tensor",
                                         buf[:, hh * 512:(hh + 1) * 512],
                                         zt[:, hh * 512:(hh + 1) * 512],
                                         buf[:, hh * 512:(hh + 1) * 512], ALU.add)
                        sp.wait(t_c)
                        t_spl = sp.op(f"sxin{it % 2}", "dma_start",
                                      out=xin_sp[s, i * 128:(i + 1) * 128, :],
                                      in_=buf)
                        x0_use[it % 2] = t_spl
                        t_comb2[psx - 2] = t_c
                        if psx == 3:
                            ps3g[0] = ps3g[1] = t_c
                        ln_stats(buf, st12, mv)
                        dve.op("tensor_tensor", sc[:, 2:3], mv[:, 0:1], mv[:, 0:1],
                               ALU.mult)
                        dve.op("tensor_tensor", sc[:, 3:4], mv[:, 1:2], sc[:, 2:3],
                               ALU.add)
                        dve.op("tensor_scalar", sc[:, 3:4], sc[:, 3:4], EPS_F,
                               None, ALU.add)
                        t_rr = dve.op("reciprocal", sc[:, 3:4], sc[:, 3:4])
                        act.wait(t_rr)
                        t_sq = act.op("sqrt", sc[:, 3:4], sc[:, 3:4])
                        dve.wait(t_sq)
                        dve.wait(S.get("h2f_guard"))
                        for hh in range(2):
                            dve.op("tensor_scalar", h2f[:, hh * 512:(hh + 1) * 512],
                                   buf[:, hh * 512:(hh + 1) * 512], sc[:, 3:4],
                                   None, ALU.mult)
                            t_h2 = dve.op("tensor_tensor",
                                          h2f[:, hh * 512:(hh + 1) * 512],
                                          h2f[:, hh * 512:(hh + 1) * 512],
                                          ln2_bc[:, hh * 512:(hh + 1) * 512],
                                          ALU.mult)
                        pe.wait(t_h2)
                        pe.wait(S.get("rt_ps_guard"))
                        pe.wait(pool_g[0])
                        pe.wait(pool_g[1])
                        for c in range(NCH):
                            pst = PS[0] if c < 4 else PS[1]
                            t_ht = pe.op("transpose",
                                         pst[:, (c % 4) * 128:(c % 4 + 1) * 128],
                                         h2f[:, c * 128:(c + 1) * 128], identf[:])
                        dve.wait(t_ht)
                        dve.op("tensor_copy", h2T[:, 0:512], PS[0][:, 0:512])
                        t_he = dve.op("tensor_copy", h2T[:, 512:1024],
                                      PS[1][:, 0:512])
                        S["h2f_guard"] = t_he
                        pool_g[0] = pool_g[1] = t_he
                        pe.wait(t_he)
                        for c in range(NCH):
                            t_lg = pe.op("matmul", PS[0][0:8, 0:128],
                                         wr_sb[:, c * 8:(c + 1) * 8],
                                         h2T[:, c * 128:(c + 1) * 128],
                                         start=(c == 0), stop=(c == NCH - 1))
                        dve.wait(t_lg)
                        t_le = dve.op("tensor_copy", lgs, PS[0][0:8, 0:128])
                        pe.wait(t_le)
                        t_lt = pe.op("transpose", PS[1][:, 0:8], lgs,
                                     identf[0:8, 0:8])
                        dve.wait(t_lt)
                        dve.op("tensor_tensor", m8[:, 0:8], PS[1][:, 0:8],
                               br_bc[:], ALU.add)
                        dve.op("reduce_max", sc[:, 4:5], m8[:, 0:8], AX.X)
                        dve.op("tensor_scalar", m8[:, 8:16], m8[:, 0:8],
                               sc[:, 4:5], None, ALU.is_ge)
                        dve.op("tensor_scalar", m8[:, 16:24], m8[:, 8:16],
                               1e30, 1e30, ALU.mult, ALU.subtract)
                        mkb = m8[:, 0:8].bitcast(BF16)
                        t_mk = dve.op("tensor_copy", mkb[:, 0:8], m8[:, 8:16])
                        t_pn = dve.op("tensor_copy", mkb[:, 8:16], m8[:, 16:24])
                        pe.wait(t_pn)
                        t_mt1 = pe.op("transpose",
                                      PS[1][0:8, 128:192].bitcast(BF16),
                                      mkb[:, 0:8], identb[:])
                        t_mt2 = pe.op("transpose",
                                      PS[1][0:8, 192:256].bitcast(BF16),
                                      mkb[:, 8:16], identb[:])
                        dve.wait(t_mt1)
                        dve.wait(t_mt2)
                        dve.op("tensor_copy", mask_fm[:, i * 128:(i + 1) * 128],
                               PS[1][0:8, 128:192].bitcast(BF16))
                        t_pe2 = dve.op("tensor_copy",
                                       pen_fm[:, i * 128:(i + 1) * 128],
                                       PS[1][0:8, 192:256].bitcast(BF16))
                        S["rt_ps_guard"] = t_pe2
                        pool_g[0] = pool_g[1] = t_pe2
            P.tap(f"Ogrp_{blk}", O_grp[:], dve.tok())
            P.tap(f"z_{blk}", z_sb[:], dve.tok())

        def phase4(s):
            wqf = arb(O_W, 8192)
            wkf = arb(O_W + 4096, 2048)
            wvf = arb(O_W + 5120, 2048)
            t_wq = pl.op("w0", "dma_start",
                         out=wqf.rearrange("p (c n) -> p c n", n=DM),
                         in_=wfq_d.rearrange("(c p) n -> p c n", p=128))
            t_wk = pl.op("w1", "dma_start",
                         out=wkf.rearrange("p (c n) -> p c n", n=256),
                         in_=wfk_d.rearrange("(c p) n -> p c n", p=128))
            t_wv = pl.op("wg", "dma_start",
                         out=wvf.rearrange("p (c n) -> p c n", n=256),
                         in_=wfv_d.rearrange("(c p) n -> p c n", p=128))
            t_wo = pl.op("w3", "dma_start",
                         out=arb(O_W + 8192, 8192).rearrange(
                             "p (c n) -> p c n", n=DM),
                         in_=wfo_d.rearrange("(c p) n -> p c n", p=128))
            S["t_wz_F"] = t_wo
            x1b = [arf(O_SCR, 1024), arf(O_SCR + 1024, 1024)]
            tmp = arf(O_SCR + 2048, 512)
            hbf = arb(O_SCR + 2560, 1024)
            hfm_t = arb(O_SCR + 3072, 1024)
            qtok = arf(O_SCR + 3584, 1024)
            qrope = arf(O_SCR + 4608, 1024)
            qbf = arb(O_SCR + 5632, 1024)
            ktok = arf(O_SCR + 6144, 256)
            krope = arf(O_SCR + 6400, 256)
            kbf = arb(O_SCR + 6656, 256)
            st12 = arf(O_SCR + 6784, 12)
            mv = arf(O_SCR + 6796, 2)
            sc = arf(O_SCR + 6798, 6)
            q_pk = arb(O_QPK, 16384)
            kF = arb(O_KPK, 8192)
            vaugF = arb(O_KPK + 4096, 16 * NKV * 65).rearrange(
                "p (j h d) -> p j h d", h=NKV, d=65)

            def rope(src_f, dst_f, nh, i):
                sv = src_f.rearrange("p (h d) -> p h d", d=HD)
                dv = dst_f.rearrange("p (h d) -> p h d", d=HD)
                tv = tmp.rearrange("p (h d) -> p h d", d=32)[:, 0:nh, :]
                cosv = cos_t[:, i * 32:(i + 1) * 32].rearrange(
                    "p d -> p () d").broadcast_to((128, nh, 32))
                sinv = sin_t[:, i * 32:(i + 1) * 32].rearrange(
                    "p d -> p () d").broadcast_to((128, nh, 32))
                x1v, x2v = sv[:, :, 0:32], sv[:, :, 32:64]
                dve.op("tensor_tensor", dv[:, :, 0:32], x1v, cosv, ALU.mult)
                dve.op("tensor_tensor", tv, x2v, sinv, ALU.mult)
                dve.op("tensor_tensor", dv[:, :, 0:32], dv[:, :, 0:32], tv,
                       ALU.subtract)
                dve.op("tensor_tensor", dv[:, :, 32:64], x1v, sinv, ALU.mult)
                dve.op("tensor_tensor", tv, x2v, cosv, ALU.mult)
                return dve.op("tensor_tensor", dv[:, :, 32:64], dv[:, :, 32:64],
                              tv, ALU.add)

            def qknorm(rsrc, dst_bf, sq_buf, nh, wn_bc, scale2):
                rv = rsrc.rearrange("p (h d) -> p h d", d=HD)
                dve.op("tensor_tensor", sq_buf[:, 0:nh * HD], rsrc, rsrc, ALU.mult)
                dve.op("tensor_reduce", sc[:, 0:nh] if nh <= 6 else ms16,
                       sq_buf[:, 0:nh * HD].rearrange("p (h d) -> p h d", d=HD),
                       AX.X, ALU.add)
                msx = sc[:, 0:nh] if nh <= 6 else ms16
                dve.op("tensor_scalar", msx, msx, 1.0 / HD, EPS_F,
                       ALU.mult, ALU.add)
                t = dve.op("reciprocal", msx, msx)
                act.wait(t)
                t_s = act.op("activation", msx, msx, AF.Sqrt, scale=float(scale2))
                dve.wait(t_s)
                dve.op("tensor_tensor", sq_buf[:, 0:nh * HD].rearrange(
                    "p (h d) -> p h d", d=HD), rv,
                    msx.rearrange("p h -> p h ()").broadcast_to((128, nh, HD)),
                    ALU.mult)
                return dve.op(
                    "tensor_tensor", dst_bf.rearrange("p (h d) -> p h d", d=HD),
                    sq_buf[:, 0:nh * HD].rearrange("p (h d) -> p h d", d=HD),
                    wn_bc.rearrange("p d -> p () d").broadcast_to((128, nh, HD)),
                    ALU.mult)

            ms16 = arf(O_SCR + 6804, 16)
            x1_use = [None, None]
            hfm_guard = None
            qk_ps = [None, None]
            v_ps = None
            tr_ev = [None, None]
            trq_ev = [None, None]
            qbf_guard = None
            kbf_guard = None
            for i in range(NT):
                buf = x1b[i % 2]
                sp.wait(x1_use[i % 2])
                t_x = sp.op(f"x0{i % 2}", "dma_start", out=buf,
                            in_=x1_sp[s, i * 128:(i + 1) * 128, :])
                dve.wait(t_x)
                ln_stats(buf, st12, mv)
                t_c = entry_norm(buf, mv, sc, EPS_F)
                dve.wait(hfm_guard)
                dve.wait(S.get("p4_hfm_pe"))
                for hh in range(2):
                    dve.op("tensor_scalar", tmp, buf[:, hh * 512:(hh + 1) * 512],
                           mv[:, 0:1], sc[:, 0:1], ALU.subtract, ALU.mult)
                    t_h = dve.op("tensor_tensor",
                                 hbf[:, hh * 512:(hh + 1) * 512], tmp,
                                 ln1_bc[:, hh * 512:(hh + 1) * 512], ALU.mult)
                x1_use[i % 2] = t_h
                for c in range(NCH):
                    slot = c % 2
                    trp = (PS[3][:, 0:64] if slot == 0
                           else PS[3][:, 512:576]).bitcast(BF16)
                    pe.wait(t_h)
                    pe.wait(tr_ev[slot])
                    t_t = pe.op("transpose", trp, hbf[:, c * 128:(c + 1) * 128],
                                identb[:])
                    dve.wait(t_t)
                    tr_ev[slot] = dve.op(
                        "tensor_copy", hfm_t[:, c * 128:(c + 1) * 128], trp)
                hfm_guard = t_t
                pe.wait(tr_ev[0])
                pe.wait(tr_ev[1])
                pe.wait(t_wq)
                pe.wait(t_wk)
                pe.wait(t_wv)
                for dg in range(2):
                    ps = PS[0][:, dg * 512:(dg + 1) * 512]
                    pe.wait(qk_ps[dg])
                    pe.op("matmul", ps, ones_bf[0:1, 0:128],
                          bq_bf[0:1, dg * 512:(dg + 1) * 512],
                          start=True, stop=False)
                    for c in range(NCH):
                        t_mm = pe.op("matmul", ps, hfm_t[:, c * 128:(c + 1) * 128],
                                     wqf[:, c * DM + dg * 512:
                                         c * DM + (dg + 1) * 512],
                                     start=False, stop=(c == NCH - 1))
                    dve.wait(t_mm)
                    qk_ps[dg] = dve.op("tensor_copy",
                                       qtok[:, dg * 512:(dg + 1) * 512], ps)
                psk = PS[1][:, 0:256]
                pe.wait(S.get("p4_kps"))
                pe.op("matmul", psk, ones_bf[0:1, 0:128], bk_bf[0:1, :],
                      start=True, stop=False)
                for c in range(NCH):
                    t_mm = pe.op("matmul", psk, hfm_t[:, c * 128:(c + 1) * 128],
                                 wkf[:, c * 256:(c + 1) * 256],
                                 start=False, stop=(c == NCH - 1))
                dve.wait(t_mm)
                S["p4_kps"] = dve.op("tensor_copy", ktok, psk)
                psv = PS[1][:, 512:768]
                pe.wait(v_ps)
                pe.op("matmul", psv, ones_bf[0:1, 0:128], bv_bf[0:1, :],
                      start=True, stop=False)
                for c in range(NCH):
                    t_mm = pe.op("matmul", psv, hfm_t[:, c * 128:(c + 1) * 128],
                                 wvf[:, c * 256:(c + 1) * 256],
                                 start=False, stop=(c == NCH - 1))
                dve.wait(t_mm)
                v_ps = dve.op("tensor_copy", vaugF[:, i, :, 0:64],
                              psv.rearrange("p (h d) -> p h d", d=HD))
                S["p4_hfm_pe"] = t_mm
                rope(qtok, qrope, NH, i)
                dve.wait(qbf_guard)
                t_qb = qknorm(qrope, qbf, qtok, NH, qn_bc, 0.125 * 0.125)
                rope(ktok, krope, NKV, i)
                dve.wait(kbf_guard)
                t_kb = qknorm(krope, kbf, ktok, NKV, kn_bc, 1.0)
                for jp in range(NH // 2):
                    slot = jp % 2
                    trp = (PS[2][:, 0:64] if slot == 0
                           else PS[2][:, 512:576]).bitcast(BF16)
                    pe.wait(t_qb)
                    pe.wait(trq_ev[slot])
                    pe.op("transpose", trp[0:64, :],
                          qbf[:, (2 * jp) * HD:(2 * jp + 1) * HD], identb[:])
                    t_t = pe.op("transpose", trp[64:128, :],
                                qbf[:, (2 * jp + 1) * HD:(2 * jp + 2) * HD],
                                identb[:], tile_position=(0, 64))
                    dve.wait(t_t)
                    trq_ev[slot] = dve.op(
                        "tensor_copy",
                        q_pk[:, jp * 2048 + i * 128: jp * 2048 + (i + 1) * 128],
                        trp)
                qbf_guard = t_t
                for kh in range(NKV):
                    slot = kh % 2
                    trp = (PS[2][:, 0:64] if slot == 0
                           else PS[2][:, 512:576]).bitcast(BF16)
                    pe.wait(t_kb)
                    pe.wait(trq_ev[slot])
                    pe.op("transpose", trp[0:64, :],
                          kbf[:, kh * HD:(kh + 1) * HD], identb[:])
                    t_t = pe.op("transpose", trp[64:128, :],
                                kbf[:, kh * HD:(kh + 1) * HD], identb[:],
                                tile_position=(0, 64))
                    dve.wait(t_t)
                    trq_ev[slot] = dve.op(
                        "tensor_copy",
                        kF[:, kh * 2048 + i * 128: kh * 2048 + (i + 1) * 128],
                        trp)
                kbf_guard = t_t
            dve.wait(v_ps)
            dve.op("memset", vaugF[:, :, :, HD:], 1.0)
            P.tap("qF_pk", q_pk[:], dve.tok())
            P.tap("kF_pk", kF[:], dve.tok())
            P.tap("vF_aug", arb(O_KPK + 4096, 16 * NKV * 65)[:], dve.tok())

        def phase7(s):
            h2fm = arb(O_QPK, 8192)
            w1a = [arb(O_QPK + 4096, 4096), arb(O_QPK + 6144, 4096)]
            w1b = [arb(O_VAG, 4096), arb(O_VAG + 2048, 4096)]
            w2sb2 = [arb(O_KPK, 16384), arb(O_W, 16384)]
            actb = arb(O_VAG + 4096, 8192)
            y_q = arf(O_W + 8192, 4096)
            pen = [arb(O_SCR, 512, p=1), arb(O_SCR + 256, 512, p=1)]
            sg = [arf(O_SCR + 512, 512), arf(O_SCR + 1024, 512),
                  arf(O_SCR + 1536, 512), arf(O_SCR + 2048, 512)]
            xinb = arf(O_SCR + 2560, 1024)
            x1b = arf(O_SCR + 3584, 1024)
            h2bf = arb(O_SCR + 4608, 1024)
            tmp = arf(O_SCR + 5120, 512)
            st12 = arf(O_SCR + 5632, 12)
            mv = arf(O_SCR + 5644, 2)
            sc = arf(O_SCR + 5646, 6)

            xin_use = None
            x1_use = None
            h2_guard = None
            tr_ev = [None, None]
            u1g = [None, None, None, None]
            u2g = [None, None, None, None]
            sg_use = [None, None, None, None]
            act_guard = {}
            y_ps = [None, None]
            yq_guard = [None, None]
            out_guard2 = [None, None]
            pen_use = [None, None]
            b1_use = [None, None]
            w2_use = [None, None]
            wa_use = [None, None]
            wb_use = [None, None]
            for tq in range(4):
                dve.wait(w2_use[0])
                dve.wait(w2_use[1])
                yq_guard = [pe.tok(), pe.tok()]
                for it in range(4):
                    i = 4 * tq + it
                    sp.wait(xin_use)
                    t_x = sp.op("xq", "dma_start", out=xinb,
                                in_=xin_sp[s, i * 128:(i + 1) * 128, :])
                    dve.wait(t_x)
                    ln_stats(xinb, st12, mv)
                    dve.op("tensor_tensor", sc[:, 2:3], mv[:, 0:1], mv[:, 0:1],
                           ALU.mult)
                    dve.op("tensor_tensor", sc[:, 3:4], mv[:, 1:2], sc[:, 2:3],
                           ALU.add)
                    dve.op("tensor_scalar", sc[:, 3:4], sc[:, 3:4], EPS_F,
                           None, ALU.add)
                    t_rr = dve.op("reciprocal", sc[:, 3:4], sc[:, 3:4])
                    act.wait(t_rr)
                    t_sq = act.op("sqrt", sc[:, 3:4], sc[:, 3:4])
                    dve.wait(t_sq)
                    dve.wait(h2_guard)
                    for hh in range(2):
                        dve.op("tensor_scalar", tmp,
                               xinb[:, hh * 512:(hh + 1) * 512], sc[:, 3:4],
                               None, ALU.mult)
                        t_h2 = dve.op("tensor_tensor",
                                      h2bf[:, hh * 512:(hh + 1) * 512], tmp,
                                      ln2_bc[:, hh * 512:(hh + 1) * 512],
                                      ALU.mult)
                    xin_use = t_h2
                    for c in range(NCH):
                        slot = c % 2
                        trp = (PS[3][:, 0:64] if slot == 0
                               else PS[3][:, 512:576]).bitcast(BF16)
                        pe.wait(t_h2)
                        pe.wait(tr_ev[slot])
                        pe.wait(u2g[2 + slot])
                        t_t = pe.op("transpose", trp,
                                    h2bf[:, c * 128:(c + 1) * 128], identb[:])
                        dve.wait(t_t)
                        tr_ev[slot] = dve.op(
                            "tensor_copy",
                            h2fm[:, c * 512 + it * 128: c * 512 + (it + 1) * 128],
                            trp)
                    h2_guard = t_t
                for e in range(8):
                    es = e % 2
                    sp.wait(pen_use[es])
                    t_pen = sp.op(f"pen{es}", "dma_start", out=pen[es][0:1, :],
                                  in_=pen_fm[e:e + 1, tq * 512:(tq + 1) * 512])
                    sp.wait(b1_use[es])
                    t_b1 = sp.op(f"b1{es}", "dma_start", out=b1c[es][:],
                                 in_=b1c_d[e])
                    w2sb = w2sb2[es]
                    pl.wait(w2_use[es])
                    t_w2 = pl.op(f"w2s{es}", "dma_start",
                                 out=w2sb.rearrange("p (c n) -> p c n", n=DM),
                                 in_=w2_d[e].rearrange("(c p) n -> p c n", p=128))
                    for gp in range(4):
                        ws = gp % 2
                        pl.wait(wa_use[ws])
                        t_wa = pl.op(f"wa{ws}", "dma_start",
                                     out=w1a[ws].rearrange("p (c n) -> p c n",
                                                           n=512),
                                     in_=w1_d[e][:, gp * 512:(gp + 1) * 512]
                                     .rearrange("(c p) n -> p c n", p=128))
                        pl.wait(wb_use[ws])
                        t_wb = pl.op(f"wb{ws}", "dma_start",
                                     out=w1b[ws].rearrange("p (c n) -> p c n",
                                                           n=512),
                                     in_=w1_d[e][:, HID + gp * 512:
                                                 HID + (gp + 1) * 512]
                                     .rearrange("(c p) n -> p c n", p=128))
                        for fc in range(4):
                            hc = gp * 4 + fc
                            ps1 = (PS[0] if fc < 2 else PS[2])[
                                :, (fc % 2) * 512:(fc % 2) * 512 + 512]
                            pe.wait(u1g[fc])
                            pe.wait(t_wa)
                            pe.wait(tr_ev[0])
                            pe.wait(tr_ev[1])
                            for c in range(NCH):
                                t_u1 = pe.op("matmul", ps1,
                                             w1a[ws][:, c * 512 + fc * 128:
                                                     c * 512 + (fc + 1) * 128],
                                             h2fm[:, c * 512:(c + 1) * 512],
                                             start=(c == 0), stop=(c == NCH - 1))
                            ps2 = (PS[1] if fc < 2 else PS[3])[
                                :, (fc % 2) * 512:(fc % 2) * 512 + 512]
                            pe.wait(u2g[fc])
                            pe.wait(t_wb)
                            pe.wait(t_pen)
                            pe.op("matmul", ps2, ones_bf[0:1, 0:128],
                                  pen[es][0:1, :], start=True, stop=False)
                            for c in range(NCH):
                                t_u2 = pe.op("matmul", ps2,
                                             w1b[ws][:, c * 512 + fc * 128:
                                                     c * 512 + (fc + 1) * 128],
                                             h2fm[:, c * 512:(c + 1) * 512],
                                             start=False, stop=(c == NCH - 1))
                            act.wait(t_u2)
                            act.wait(t_b1)
                            act.wait(sg_use[fc])
                            t_sg = act.op("activation", sg[fc], ps2,
                                          AF.Sigmoid,
                                          bias=b1c[es][:, 16 + hc:17 + hc])
                            u2g[fc] = t_sg
                            dve.wait(t_sg)
                            dve.wait(t_u1)
                            dve.wait(act_guard.get(hc))
                            t_ac = dve.op("scalar_tensor_tensor",
                                          actb[:, hc * 512:(hc + 1) * 512],
                                          ps1, b1c[es][:, hc:hc + 1], sg[fc],
                                          ALU.add, ALU.mult)
                            u1g[fc] = t_ac
                            sg_use[fc] = t_ac
                            act_guard[hc] = t_ac
                        wa_use[ws] = t_u1
                        wb_use[ws] = t_u2
                    pe.wait(t_w2)
                    pe.wait(u1g[2])
                    pe.wait(u1g[3])
                    for mc in range(NCH):
                        ysl = mc % 2
                        ps = PS[2][:, ysl * 512:(ysl + 1) * 512]
                        pe.wait(y_ps[ysl])
                        first = True
                        if e == 0:
                            pe.op("matmul", ps, b2_bf[:, mc * 128:(mc + 1) * 128],
                                  mask_fm[:, tq * 512:(tq + 1) * 512],
                                  start=True, stop=False)
                            first = False
                        for hc in range(16):
                            pe.wait(act_guard[hc])
                            t_y = pe.op("matmul", ps,
                                        w2sb[:, hc * 1024 + mc * 128:
                                             hc * 1024 + (mc + 1) * 128],
                                        actb[:, hc * 512:(hc + 1) * 512],
                                        start=first, stop=(hc == 15))
                            first = False
                        dve.wait(t_y)
                        dve.wait(yq_guard[ysl])
                        if e == 0:
                            t_yq = dve.op("tensor_copy",
                                          y_q[:, mc * 512:(mc + 1) * 512], ps)
                        else:
                            t_yq = dve.op("tensor_tensor",
                                          y_q[:, mc * 512:(mc + 1) * 512], ps,
                                          y_q[:, mc * 512:(mc + 1) * 512],
                                          ALU.add)
                        y_ps[ysl] = t_yq
                        yq_guard[ysl] = t_yq
                        u1g[2 + ysl] = t_yq
                    pen_use[es] = t_u2
                    b1_use[es] = t_ac
                    w2_use[es] = t_y
                    for hc in range(16):
                        act_guard[hc] = t_y
                for it in range(4):
                    i = 4 * tq + it
                    psx = 2 if it % 2 == 0 else 3
                    pe.wait(out_guard2[psx - 2])
                    pe.wait(y_ps[0])
                    pe.wait(y_ps[1])
                    if psx == 3:
                        pe.wait(u2g[2])
                        pe.wait(u2g[3])
                    t_tr = None
                    for c in range(NCH):
                        pe.wait(yq_guard[c % 2])
                        t_tr = pe.op("transpose", PS[psx][:, c * 128:(c + 1) * 128],
                                     y_q[:, c * 512 + it * 128:
                                          c * 512 + (it + 1) * 128], identf[:])
                    sp.wait(xin_use)
                    t_xi = sp.op("xq", "dma_start", out=xinb,
                                 in_=xin_sp[s, i * 128:(i + 1) * 128, :])
                    sp.wait(x1_use)
                    t_x1 = sp.op("x1q", "dma_start", out=x1b,
                                 in_=x1_sp[s, i * 128:(i + 1) * 128, :])
                    dve.wait(t_tr)
                    dve.wait(t_xi)
                    for hh in range(2):
                        t_s = dve.op("tensor_tensor",
                                     xinb[:, hh * 512:(hh + 1) * 512],
                                     PS[psx][:, hh * 512:(hh + 1) * 512],
                                     xinb[:, hh * 512:(hh + 1) * 512], ALU.add)
                    out_guard2[psx - 2] = t_s
                    if psx == 3:
                        u2g[2] = u2g[3] = t_s
                    ln_stats(xinb, st12, mv)
                    t_rs = rstd_of(mv[:, 1:2], sc[:, 0:1], EPS_LN)
                    dve.wait(t_rs)
                    dve.wait(t_x1)
                    for hh in range(2):
                        dve.op("tensor_scalar", tmp,
                               xinb[:, hh * 512:(hh + 1) * 512],
                               mv[:, 0:1], sc[:, 0:1], ALU.subtract, ALU.mult)
                        t_o = dve.op("tensor_tensor",
                                     x1b[:, hh * 512:(hh + 1) * 512], tmp,
                                     x1b[:, hh * 512:(hh + 1) * 512], ALU.add)
                    xin_use = t_o
                    sp.wait(t_o)
                    x1_use = sp.op("out", "dma_start",
                                   out=out_d[s, i * 128:(i + 1) * 128, :], in_=x1b)

        for s in range(nseq):
            if nphases >= 1:
                phase1(s)
                P.barrier()
            if nphases >= 2:
                t_wz = pl.op("w0", "dma_start",
                             out=arb(O_W, 8192).rearrange("p (c n) -> p c n", n=DM),
                             in_=wao_d.rearrange("(c p) n -> p c n", p=128))
                S["t_wz_A"] = t_wz
                attention(s, "A")
                P.barrier()
            if nphases >= 3:
                phase4(s)
                P.barrier()
            if nphases >= 4:
                attention(s, "F")
                P.barrier()
            if nphases >= 5:
                phase7(s)
                P.barrier()

        P.barrier()
        final_sp = P.sp.toks() + P.pl.toks()

        with nc.Block() as block:
            @block.tensor
            def _(t):
                for f in P.streams["pe"]:
                    f(t)

            @block.vector
            def _(v):
                for f in P.streams["dve"]:
                    f(v)

            @block.scalar
            def _(sc_):
                for f in P.streams["act"]:
                    f(sc_)

            @block.sync
            def _(sy):
                for f in P.streams["sp"]:
                    f(sy)
                for key, val in final_sp:
                    sy.wait_ge(P.sems[key], val)

            @block.gpsimd
            def _(gp):
                for f in P.streams["pl"]:
                    f(gp)

    return nc, P


def make_inputs(core_inputs, core_id):
    d = core_inputs
    m = {}
    m["x"] = np.ascontiguousarray(
        np.asarray(d["L0_attn_tokens"], np.float32)[2 * core_id:2 * core_id + 2])
    for k in ("L0_attn_Wq", "L0_attn_Wkv", "L0_attn_Wout", "L0_attn_Wg",
              "L0_ffn_Wq", "L0_ffn_Wk", "L0_ffn_Wv", "L0_ffn_Wg", "L0_ffn_Wo",
              "L0_ffn_W1", "L0_ffn_W2", "L0_ffn_b2"):
        m[k] = np.ascontiguousarray(np.asarray(d[k], dtype=np.float32))
    m["d_normw_bc"] = np.ascontiguousarray(
        np.broadcast_to(np.asarray(d["L0_attn_norm_w"], np.float32), (128, DM)))
    m["d_ln1_bc"] = np.ascontiguousarray(
        np.broadcast_to(np.asarray(d["L0_ffn_ln1_w"], np.float32), (128, DM)))
    m["d_ln2_bc"] = np.ascontiguousarray(
        np.broadcast_to(np.asarray(d["L0_ffn_ln2_w"], np.float32), (128, DM)))
    m["d_qn_bc"] = np.ascontiguousarray(
        np.broadcast_to(np.asarray(d["L0_ffn_qn_w"], np.float32), (128, HD)))
    m["d_kn_bc"] = np.ascontiguousarray(
        np.broadcast_to(np.asarray(d["L0_ffn_kn_w"], np.float32), (128, HD)))
    m["d_br_bc"] = np.ascontiguousarray(
        np.broadcast_to(np.asarray(d["L0_ffn_br"], np.float32), (128, NE)))
    m["d_bo_col"] = np.ascontiguousarray(
        np.asarray(d["L0_ffn_bo"], np.float32).reshape(NCH, 128).T)
    m["d_bg_col"] = np.ascontiguousarray(
        np.asarray(d["L0_ffn_bg"], np.float32).reshape(HD, 1))
    m["d_b1_cols"] = np.ascontiguousarray(
        np.asarray(d["L0_ffn_b1"], np.float32).reshape(NE, 32, 128)
        .transpose(0, 2, 1))
    m["d_wr_cols"] = np.ascontiguousarray(
        np.asarray(d["L0_ffn_Wr"], np.float32).reshape(NCH, 128, NE)
        .transpose(1, 0, 2).reshape(128, 8 * NE))
    m["d_bq_row"] = np.asarray(d["L0_ffn_bq"], np.float32).reshape(1, DM).copy()
    m["d_bk_row"] = np.asarray(d["L0_ffn_bk"], np.float32).reshape(1, 256).copy()
    m["d_bv_row"] = np.asarray(d["L0_ffn_bv"], np.float32).reshape(1, 256).copy()
    m["d_identf"] = np.eye(128, dtype=np.float32)
    m["d_identb"] = np.eye(128, dtype=BFNP)
    jj, qq = np.meshgrid(np.arange(128), np.arange(128), indexing="ij")
    m["d_ulo"] = np.where(jj > qq, np.float32(NEGBIG), np.float32(0)).astype(BFNP)
    inv = (1.0 / (10000.0 ** (np.arange(0, HD, 2, dtype=np.float32)
                              / np.float32(HD)))).astype(np.float32)
    fr = np.outer(np.arange(T, dtype=np.float32), inv).astype(np.float32)
    m["d_cos_t"] = np.ascontiguousarray(
        np.cos(fr).astype(np.float32).reshape(NT, 128, 32)
        .transpose(1, 0, 2).reshape(128, NT * 32))
    m["d_sin_t"] = np.ascontiguousarray(
        np.sin(fr).astype(np.float32).reshape(NT, 128, 32)
        .transpose(1, 0, 2).reshape(128, NT * 32))
    return m


def kernel(**inputs):
    nc, P = build_program()
    in_maps = [make_inputs(inputs, c) for c in range(8)]
    res = run_bass_kernel_spmd(nc, in_maps, core_ids=list(range(8)))
    out = np.concatenate([res.results[c]["out"] for c in range(8)], axis=0)
    return np.ascontiguousarray(out.reshape(16, T, DM)).astype(np.float32)
